# version 03
# baseline (speedup 1.0000x reference)
import numpy as np
from contextlib import ExitStack

import concourse.bass as bass
import concourse.mybir as mybir
from concourse.bass_utils import run_bass_kernel_spmd

F32 = mybir.dt.float32
BF16 = mybir.dt.bfloat16
AF = mybir.ActivationFunctionType
ALU = mybir.AluOpType
AX = mybir.AxisListType

ACT_CONSTS = [0.0, 1.0, -1.0, 0.5, -0.5, 2.0, 1e-5, 1.1920928955078125e-07,
              1.0 / 64, 1.0 / 128, 1.0 / 2048, 1e6, 0.6, 0.9995, -60.0,
              1.0 / 4096, 64.0, 128.0]

DEBUG_TAP = None
S5A_PHASES = ("router", "experts", "shared")

T = 4096
D = 2048
NKC = D // 128
LN_EPS = 1e-5

ENGINES = ("sync", "gpsimd", "tensor", "vector", "scalar")
SEMS = ("LD", "CA", "PE", "DV", "AC", "ST")


class Prog:

    def __init__(self, nc, es):
        self.nc = nc
        self.es = es
        self.ops = {e: [] for e in ENGINES}
        self.sems = {}
        self.cnt = {}
        for s in SEMS:
            self.sems[s] = es.enter_context(nc.semaphore(s))
            self.cnt[s] = 0
        self._pw = {e: [] for e in ENGINES}
        self.meta = {e: [] for e in ENGINES}

    def wait(self, engine, tok):
        if tok is not None:
            self._pw[engine].append(tok)

    def _record(self, engine, fn, inc_sem=None, inc_n=1):
        waits = self._pw[engine]
        self._pw[engine] = []
        sems = self.sems

        def run(eng, waits=waits, fn=fn, inc_sem=inc_sem, inc_n=inc_n):
            seen = {}
            for (s, v) in waits:
                if seen.get(s, -1) < v:
                    seen[s] = v
            for s, v in seen.items():
                eng.wait_ge(sems[s], v)
            inst = fn(eng)
            if inc_sem is not None:
                inst.then_inc(sems[inc_sem], inc_n)
        self.ops[engine].append(run)
        wd = {}
        for (sn, v) in waits:
            wd[sn] = max(wd.get(sn, 0), v)
        self.meta[engine].append((wd, inc_sem, inc_n))
        if inc_sem is not None:
            self.cnt[inc_sem] += inc_n
            return (inc_sem, self.cnt[inc_sem])
        return None

    def ld(self, out, in_):
        return self._record("sync", lambda e, o=out, i=in_: e.dma_start(out=o, in_=i), "LD", 16)

    def st(self, out, in_):
        return self._record("sync", lambda e, o=out, i=in_: e.dma_start(out=o, in_=i), "ST", 16)

    def ca(self, out, in_):
        return self._record("gpsimd", lambda e, o=out, i=in_: e.dma_start(out=o, in_=i), "CA", 16)

    def pe(self, fn, inc=True):
        return self._record("tensor", fn, "PE" if inc else None)

    def dv(self, fn):
        tok = self._record("vector", fn, "DV")
        self.wait("vector", tok)
        return tok

    def ac(self, fn):
        return self._record("scalar", fn, "AC")

    def tok(self, sem):
        return (sem, self.cnt[sem])

    def barrier(self):
        toks = [(s, self.cnt[s]) for s in SEMS if self.cnt[s] > 0]
        for e in ENGINES:
            for t in toks:
                self.wait(e, t)
            self._record(e, lambda eng: eng.nop())

    def check_deadlock(self):
        cnt = {sname: 0 for sname in SEMS}
        idx = {e: 0 for e in ENGINES}
        total = sum(len(v) for v in self.meta.values())
        done = 0
        while done < total:
            progress = False
            for e in ENGINES:
                while idx[e] < len(self.meta[e]):
                    waits, inc_sem, inc_n = self.meta[e][idx[e]]
                    if any(cnt[sn] < v for sn, v in waits.items()):
                        break
                    if inc_sem is not None:
                        cnt[inc_sem] += inc_n
                    idx[e] += 1
                    done += 1
                    progress = True
            if not progress:
                for e in ENGINES:
                    if idx[e] < len(self.meta[e]):
                        waits, _, _ = self.meta[e][idx[e]]
                        blocked = {sn: (v, cnt[sn]) for sn, v in waits.items()
                                   if cnt[sn] < v}
                        print(f"DEADLOCK {e} op {idx[e]}/{len(self.meta[e])} "
                              f"blocked on {blocked}")
                raise RuntimeError("static deadlock detected")
        return True

    def finalize(self):
        block = self.es.enter_context(self.nc.Block())
        for e in ENGINES:
            ops = self.ops[e]
            getattr(block, e)(lambda eng, ops=ops: [f(eng) for f in ops])



def MM(p, out, l, r, start, stop, tp=None, inc=True):
    return p.pe(lambda e, o=out, l=l, r=r, a=start, b=stop, tp=tp:
                e.matmul(o, l, r, start=a, stop=b, tile_position=tp), inc=inc)


def mm_chain(p, out, pairs, tp=None, inc=True):
    n = len(pairs)
    tok = None
    for i, (l, r) in enumerate(pairs):
        tok = MM(p, out, l, r, i == 0, i == n - 1, tp=tp,
                 inc=(inc and i == n - 1))
    return tok


def TT(p, out, a, b, op):
    return p.dv(lambda e, o=out, x=a, y=b, op=op: e.tensor_tensor(o, x, y, op))


def TS(p, out, a, s1, s2, op0, op1=None):
    if s2 is None:
        return p.dv(lambda e, o=out, x=a, s=s1, op=op0:
                    e.tensor_single_scalar(o, x, s, op))
    return p.dv(lambda e, o=out, x=a, s1=s1, s2=s2, op0=op0, op1=op1:
                e.tensor_scalar(o, x, s1, s2, op0, op1))


def STT(p, out, in0, sc, in1, op0, op1):
    return p.dv(lambda e, o=out, x=in0, s=sc, y=in1, a=op0, b=op1:
                e.scalar_tensor_tensor(o, x, s, y, a, b))


def CP(p, out, in_):
    return p.dv(lambda e, o=out, i=in_: e.tensor_copy(o, i))


def ACT(p, out, in_, func, bias=0.0, scale=1.0):
    return p.ac(lambda e, o=out, i=in_, f=func, b=bias, s=scale:
                e.activation(o, i, f, bias=b, scale=s))


def SCAN(p, out, d0, d1, init, op0, op1):
    tok = p.dv(lambda e, o=out, a=d0, b=d1, i=init, x=op0, y=op1:
               e.tensor_tensor_scan(o, a, b, i, x, y))
    p.wait("vector", tok)
    return tok


def RED(p, out, in_, op):
    tok = p.dv(lambda e, o=out, i=in_, op=op: e.tensor_reduce(o, i, AX.X, op))
    p.wait("vector", tok)
    return tok


def TRN(p, out, in_, ident):
    return p.pe(lambda e, o=out, i=in_, d=ident: e.transpose(o, i, d))



def _host_consts():
    c = {}
    c["c_idn"] = np.eye(128, dtype=np.float32)
    tvec = np.arange(1024)
    c["c_m0_64"] = np.broadcast_to((tvec % 64 != 0).astype(np.float32), (128, 1024)).copy()
    c["c_m1_64"] = np.broadcast_to((tvec % 64 == 0).astype(np.float32), (128, 1024)).copy()
    c["c_mc128"] = np.broadcast_to((tvec % 128 != 0).astype(np.float32), (128, 1024)).copy()
    c["c_mc128r"] = np.broadcast_to((tvec % 128 != 127).astype(np.float32), (128, 1024)).copy()
    i = np.arange(128)
    c["c_ltri"] = (i[:, None] <= i[None, :]).astype(np.float32)
    c["c_ustrict"] = (i[:, None] > i[None, :]).astype(np.float32)
    c["c_negbig"] = np.where(i[None, :] < i[:, None], -60.0, 0.0).astype(np.float32)
    c["c_ones"] = np.ones((128, 128), dtype=np.float32)
    c["c_scal"] = np.broadcast_to(np.array(ACT_CONSTS, dtype=np.float32), (128, len(ACT_CONSTS))).copy()
    mt = np.zeros((1, 128), dtype=np.float32); mt[0, :64] = 1.0
    mb = np.zeros((1, 128), dtype=np.float32); mb[0, 64:] = 1.0
    c["c_mtop"] = mt
    c["c_mbot"] = mb
    sel = np.zeros((8, 8 * 128), dtype=np.float32)
    for e in range(8):
        sel[e, e * 128:(e + 1) * 128] = 1.0
    c["c_sel"] = sel
    return c


def _prep_weights(inp):
    w = {}
    f32 = np.float32

    def t2(a):
        return np.ascontiguousarray(a.T.astype(f32, copy=False))
    w["w0_in"] = t2(inp["L0_attn_w_in"])
    w["w0_out"] = t2(inp["L0_attn_w_out"])
    w["w0_g"] = t2(inp["L0_ffn_gate_w"])
    w["w0_u"] = t2(inp["L0_ffn_up_w"])
    w["w0_d"] = t2(inp["L0_ffn_down_w"])
    w["w0_gs"] = np.ascontiguousarray(inp["L0_ffn_gate_s"].astype(f32).reshape(44, 128).T)
    w["w0_us"] = np.ascontiguousarray(inp["L0_ffn_up_s"].astype(f32).reshape(44, 128).T)
    w["w0_ds"] = np.ascontiguousarray(inp["L0_ffn_down_s"].astype(f32).reshape(16, 128).T)
    w["w1_in"] = np.ascontiguousarray(inp["L1_attn_W_in"].astype(f32))
    w["w1_cw"] = np.ascontiguousarray(
        inp["L1_attn_conv_w"].astype(f32).reshape(48, 128, 4).transpose(1, 0, 2)
        .reshape(128, 192))
    w["w1_cb"] = np.ascontiguousarray(inp["L1_attn_conv_b"].astype(f32).reshape(48, 128).T)
    w["w1_dtb"] = inp["L1_attn_dt_bias"].astype(f32).reshape(64, 1)
    w["w1_alog"] = inp["L1_attn_A_log"].astype(f32).reshape(64, 1)
    w["w1_d64"] = np.ascontiguousarray(inp["L1_attn_D"].astype(f32).reshape(1, 64))
    w["w1_nw"] = np.ascontiguousarray(inp["L1_attn_norm_weight"].astype(f32).reshape(32, 128).T)
    w["w1_out"] = np.ascontiguousarray(inp["L1_attn_W_out"].astype(f32))
    w["w2_wg"] = np.ascontiguousarray(inp["L1_ffn_Wg"].astype(f32))
    w["w2_bg"] = inp["L1_ffn_bg"].astype(f32).reshape(1, 8)
    w["w2_wgate"] = np.ascontiguousarray(inp["L1_ffn_Wgate"].astype(f32).reshape(12288, 1024))
    w["w2_bgate"] = np.ascontiguousarray(
        inp["L1_ffn_bgate"].astype(f32).reshape(48, 128).T)
    w["w2_wup"] = np.ascontiguousarray(inp["L1_ffn_Wup"].astype(f32).reshape(12288, 1024))
    w["w2_bup"] = np.ascontiguousarray(inp["L1_ffn_bup"].astype(f32).reshape(48, 128).T)
    w["w2_wdown"] = np.ascontiguousarray(inp["L1_ffn_Wdown"].astype(f32).reshape(6144, 2048))
    w["w2_bdown"] = np.ascontiguousarray(inp["L1_ffn_bdown"].astype(f32))
    w["w2_wsu"] = np.ascontiguousarray(inp["L1_ffn_Ws_up"].astype(f32))
    w["w2_bsu"] = np.ascontiguousarray(inp["L1_ffn_bs_up"].astype(f32).reshape(16, 128).T)
    w["w2_wsd"] = np.ascontiguousarray(inp["L1_ffn_Ws_down"].astype(f32))
    w["w2_bsd"] = np.ascontiguousarray(inp["L1_ffn_bs_down"].astype(f32).reshape(16, 128).T)
    w.update(_host_consts())
    return w


WSPECS = {
    "w0_in": ([D, 10240], F32), "w0_out": ([D, D], F32),
    "w0_g": ([D, 5632], F32), "w0_u": ([D, 5632], F32), "w0_d": ([5632, D], F32),
    "w0_gs": ([128, 44], F32), "w0_us": ([128, 44], F32), "w0_ds": ([128, 16], F32),
    "w1_in": ([D, 10304], F32), "w1_cw": ([128, 192], F32), "w1_cb": ([128, 48], F32),
    "w1_dtb": ([64, 1], F32), "w1_alog": ([64, 1], F32), "w1_d64": ([1, 64], F32),
    "w1_nw": ([128, 32], F32), "w1_out": ([4096, D], F32),
    "w2_wg": ([D, 8], F32), "w2_bg": ([1, 8], F32),
    "w2_wgate": ([12288, 1024], F32), "w2_bgate": ([128, 48], F32),
    "w2_wup": ([12288, 1024], F32), "w2_bup": ([128, 48], F32),
    "w2_wdown": ([6144, D], F32), "w2_bdown": ([6, D], F32),
    "w2_wsu": ([D, D], F32), "w2_bsu": ([128, 16], F32),
    "w2_wsd": ([D, D], F32), "w2_bsd": ([128, 16], F32),
    "c_idn": ([128, 128], F32),
    "c_m0_64": ([128, 1024], F32), "c_m1_64": ([128, 1024], F32),
    "c_mc128": ([128, 1024], F32), "c_mc128r": ([128, 1024], F32),
    "c_ltri": ([128, 128], F32), "c_ustrict": ([128, 128], F32),
    "c_negbig": ([128, 128], F32), "c_ones": ([128, 128], F32),
    "c_sel": ([8, 1024], F32), "c_mtop": ([1, 128], F32), "c_mbot": ([1, 128], F32),
    "c_scal": ([128, len(ACT_CONSTS)], F32),
}

SCRATCH = {
    "x1": ([D, T], F32), "x2": ([D, T], F32), "x3": ([D, T], F32),
    "zs": ([T, 4096], BF16),
    "xbc": ([6144, T + 3], BF16),
    "dtr": ([64, T], F32),
    "hxp": ([8192, T], BF16),
    "wfm": ([8, T], F32),
    "wout_bf": ([4096, D], BF16),
}


class G:
    pass


def _setup_globals(p, d, es):
    nc = p.nc
    g = G()
    g.d = d
    g.ps = [es.enter_context(nc.psum_tensor(f"psb{i}", [128, 512], F32))
            for i in range(8)]

    def sb(name, shape, dt=F32):
        return es.enter_context(nc.sbuf_tensor(name, shape, dt))
    g.sb = sb
    g.idn = sb("g_idn", [128, 128])
    g.idnb = sb("g_idnb", [128, 128], BF16)
    g.onesb = sb("g_onesb", [128, 128], BF16)
    g.ones = sb("g_ones", [128, 128])
    g.ltri = sb("g_ltri", [128, 128])
    g.ustrict = sb("g_ustrict", [128, 128])
    g.negbig = sb("g_negbig", [128, 128])
    g.m0_64 = sb("g_m0", [128, 1024])
    g.m1_64 = sb("g_m1", [128, 1024])
    g.mc128 = sb("g_mc", [128, 1024])
    g.mc128r = sb("g_mcr", [128, 1024])
    g.rows = sb("g_rows", [1, 4096])
    g.scal = sb("g_scal", [128, len(ACT_CONSTS)])
    g.scalb = sb("g_scalb", [128, len(ACT_CONSTS)], BF16)
    p.ld(g.idn[:], d["c_idn"][:])
    p.ca(g.idnb[:], d["c_idn"][:])
    p.ca(g.onesb[:], d["c_ones"][:])
    p.ld(g.ones[:], d["c_ones"][:])
    p.ld(g.ltri[:], d["c_ltri"][:])
    p.ld(g.ustrict[:], d["c_ustrict"][:])
    p.ld(g.negbig[:], d["c_negbig"][:])
    p.ld(g.m0_64[:], d["c_m0_64"][:])
    p.ld(g.m1_64[:], d["c_m1_64"][:])
    p.ld(g.mc128[:], d["c_mc128"][:])
    p.ld(g.mc128r[:], d["c_mc128r"][:])
    p.ld(g.scal[:], d["c_scal"][:])
    p.ca(g.scalb[:], d["c_scal"][:])
    for j, v in enumerate(ACT_CONSTS):
        nc.const_aps.register(F32, v, g.scal[:, j:j + 1])
        nc.const_aps.register(BF16, v, g.scalb[:, j:j + 1])
    p.barrier()
    return g



def ln_stats_chunk(p, g, src_b16, sqb, W, first, last, src_tok=None, war=None):
    if src_tok is not None:
        p.wait("scalar", src_tok)
    if war is not None:
        p.wait("scalar", war)
    t_sq = p.ac(lambda e, o=sqb[:, 0:W], i=src_b16[:, 0:W]:
                e.activation(o, i, AF.Square))
    tok = None
    for s in range(W // 512):
        if src_tok is not None:
            p.wait("tensor", src_tok)
        MM(p, g.ps[s][0:1, :], g.onesb[:, 0:1], src_b16[:, s * 512:(s + 1) * 512],
           first, last, inc=False)
        p.wait("tensor", t_sq)
        tok = MM(p, g.ps[2 + s][0:1, :], g.onesb[:, 0:1], sqb[:, s * 512:(s + 1) * 512],
                 first, last, inc=True)
    return tok


def ln_rows_and_bcast(p, g, W, n_feat, stat_tok):
    inv = 1.0 / n_feat
    R = g.rows
    tok = None
    for s in range(W // 512):
        mu = R[0:1, 0:512]
        ms = R[0:1, 512:1024]
        mu2 = R[0:1, 1024:1536]
        var = R[0:1, 1536:2048]
        rstd = R[0:1, 2048 + s * 512:2560 + s * 512]
        nmr = R[0:1, 3072 + s * 512:3584 + s * 512]
        p.wait("vector", stat_tok)
        t_mu = TS(p, mu, g.ps[s][0:1, :], inv, None, ALU.mult)
        t_ms = TS(p, ms, g.ps[2 + s][0:1, :], inv, None, ALU.mult)
        p.wait("scalar", t_mu)
        t_mu2 = ACT(p, mu2, mu, AF.Square)
        p.wait("vector", t_mu2)
        t_var = TT(p, var, ms, mu2, ALU.subtract)
        p.wait("scalar", t_var)
        ACT(p, rstd, var, AF.Ln, bias=LN_EPS)
        t_rs = ACT(p, rstd, rstd, AF.Exp, scale=-0.5)
        p.wait("vector", t_rs)
        t_nmr = STT(p, nmr, mu, -1.0, rstd, ALU.mult, ALU.mult)
        p.wait("tensor", t_rs)
        p.wait("tensor", t_nmr)
        MM(p, g.ps[4 + s][:, :], g.ones[0:1, :], rstd, True, True, inc=False)
        tok = MM(p, g.ps[6 + s][:, :], g.ones[0:1, :], nmr, True, True)
    return tok


def ln_norm_chunk(p, g, out_b16, src, scr, W, bc_tok):
    tok = None
    for s in range(W // 512):
        p.wait("vector", bc_tok)
        TT(p, scr[:, 0:512], src[:, s * 512:(s + 1) * 512], g.ps[4 + s][:, :], ALU.mult)
        tok = TT(p, out_b16[:, s * 512:(s + 1) * 512], scr[:, 0:512],
                 g.ps[6 + s][:, :], ALU.add)
    return tok


def ln_resid_chunk(p, g, xnew, src, xr, scr, W, bc_tok):
    tok = None
    for s in range(W // 512):
        p.wait("vector", bc_tok)
        TT(p, scr[:, 0:512], src[:, s * 512:(s + 1) * 512], g.ps[4 + s][:, :], ALU.mult)
        TT(p, scr[:, 0:512], scr[:, 0:512], g.ps[6 + s][:, :], ALU.add)
        tok = TT(p, xnew[:, s * 512:(s + 1) * 512], scr[:, 0:512],
                 xr[:, s * 512:(s + 1) * 512], ALU.add)
    return tok


def w_rearr(dram_ap, col0, ncols):
    return dram_ap[:, col0:col0 + ncols].rearrange("(c p) o -> p c o", p=128)



def emit_s2(p, g):
    d = g.d
    nc = p.nc
    W = 1024
    NIC = 44
    with ExitStack() as es:
        xln = [es.enter_context(nc.sbuf_tensor(f"s2_xln{k}", [128, W], BF16))
               for k in range(NKC)]
        sqb = es.enter_context(nc.sbuf_tensor("s2_sq", [128, W], BF16))
        h = [es.enter_context(nc.sbuf_tensor(f"s2_h{i}", [128, W], BF16))
             for i in range(NIC)]
        scr = es.enter_context(nc.sbuf_tensor("s2_scr", [128, 512], F32))
        hg = es.enter_context(nc.sbuf_tensor("s2_hg", [128, 512], F32))
        wg_b = [es.enter_context(nc.sbuf_tensor(f"s2_wg{i}", [128, 2048], BF16)) for i in range(2)]
        wu_b = [es.enter_context(nc.sbuf_tensor(f"s2_wu{i}", [128, 2048], BF16)) for i in range(2)]
        wd_b = [es.enter_context(nc.sbuf_tensor("s2_wd0", [128, NIC * 128], BF16))]
        gs = es.enter_context(nc.sbuf_tensor("s2_gs", [128, 44], F32))
        us = es.enter_context(nc.sbuf_tensor("s2_us", [128, 44], F32))
        ds = es.enter_context(nc.sbuf_tensor("s2_ds", [128, 16], F32))
        delta = xln
        xres = [es.enter_context(nc.sbuf_tensor(f"s2_xr{i}", [128, W], F32)) for i in range(2)]
        xnew = es.enter_context(nc.sbuf_tensor("s2_xn", [128, W], F32))

        p.ld(gs[:], d["w0_gs"][:])
        p.ld(us[:], d["w0_us"][:])
        p.ld(ds[:], d["w0_ds"][:])
        p.barrier()

        for tt in range(T // W):
            c0 = tt * W
            toks = [p.ca(xln[k][:], d["x1"][k * 128:(k + 1) * 128, c0:c0 + W])
                    for k in range(NKC)]
            st_tok = None
            for k in range(NKC):
                st_tok = ln_stats_chunk(p, g, xln[k], sqb, W, k == 0, k == NKC - 1,
                                        src_tok=toks[k], war=st_tok)
            bc = ln_rows_and_bcast(p, g, W, D, st_tok)
            for k in range(NKC):
                ln_norm_chunk(p, g, xln[k], xln[k], scr, W, bc)
            p.barrier()

            ev_toks = {}
            hg_guard = {}
            mm_toks = {}
            for ic in range(NIC):
                bsel = ic % 2
                if ic >= 2:
                    p.wait("gpsimd", mm_toks[ic - 2])
                wg_t = p.ca(wg_b[bsel][:].rearrange("p (c o) -> p c o", o=128),
                            w_rearr(d["w0_g"], ic * 128, 128))
                wu_t = p.ca(wu_b[bsel][:].rearrange("p (c o) -> p c o", o=128),
                            w_rearr(d["w0_u"], ic * 128, 128))
                bks = [g.ps[4 * bsel + j] for j in range(4)]
                if ic >= 2:
                    p.wait("tensor", ev_toks[ic - 2])
                p.wait("tensor", wg_t)
                p.wait("tensor", wu_t)
                tok_u = None
                for s in range(2):
                    mm_chain(p, bks[s][:, :],
                             [(wg_b[bsel][:, kc * 128:(kc + 1) * 128],
                               xln[kc][:, s * 512:(s + 1) * 512]) for kc in range(NKC)],
                             inc=False)
                    tok_u = mm_chain(p, bks[2 + s][:, :],
                                     [(wu_b[bsel][:, kc * 128:(kc + 1) * 128],
                                       xln[kc][:, s * 512:(s + 1) * 512]) for kc in range(NKC)])
                last_tok = hg_guard.get("tok")
                for s in range(2):
                    p.wait("scalar", tok_u)
                    if last_tok is not None:
                        p.wait("scalar", last_tok)
                    t_hg = p.ac(lambda e, o=hg[:, 0:512], i=bks[s][:, :], sc=gs[:, ic:ic + 1]:
                                e.activation(o, i, AF.Silu, scale=sc))
                    p.wait("vector", t_hg)
                    last_tok = STT(p, h[ic][:, s * 512:(s + 1) * 512], bks[2 + s][:, :],
                                   us[:, ic:ic + 1], hg[:, 0:512], ALU.mult, ALU.mult)
                ev_toks[ic] = last_tok
                hg_guard["tok"] = last_tok
                mm_toks[ic] = tok_u
            p.barrier()

            dn_toks = {}
            for oc in range(NKC):
                bsel = 0
                if oc >= 1:
                    p.wait("gpsimd", prev_mm_tok)
                wd_t = p.ca(wd_b[bsel][:].rearrange("p (c o) -> p c o", o=128),
                            w_rearr(d["w0_d"], oc * 128, 128))
                bks = [g.ps[2 * (oc % 2) + j] for j in range(2)]
                if oc >= 2:
                    p.wait("tensor", dn_toks[oc - 2])
                p.wait("tensor", wd_t)
                toks_d = []
                for s in range(2):
                    toks_d.append(mm_chain(p, bks[s][:, :],
                                           [(wd_b[bsel][:, kc * 128:(kc + 1) * 128],
                                             h[kc][:, s * 512:(s + 1) * 512]) for kc in range(NIC)]))
                prev_mm_tok = toks_d[1]
                t_ev = None
                for s in range(2):
                    p.wait("scalar", toks_d[s])
                    t_ev = p.ac(lambda e, o=delta[oc][:, s * 512:(s + 1) * 512],
                                i=bks[s][:, :], sc=ds[:, oc:oc + 1]:
                                e.activation(o, i, AF.Copy, scale=sc))
                dn_toks[oc] = t_ev
            p.barrier()

            if DEBUG_TAP == "s2_delta":
                for k in range(NKC):
                    p.st(d["t_dbg"][k * 128:(k + 1) * 128, c0:c0 + W], delta[k][:])
                p.barrier()

            st_tok = None
            for k in range(NKC):
                st_tok = ln_stats_chunk(p, g, delta[k], sqb, W, k == 0, k == NKC - 1,
                                        war=st_tok)
            bc = ln_rows_and_bcast(p, g, W, D, st_tok)
            for k in range(NKC):
                xr = xres[k % 2]
                t_x = p.ld(xr[:], d["x1"][k * 128:(k + 1) * 128, c0:c0 + W])
                p.wait("vector", t_x)
                t_n = ln_resid_chunk(p, g, xnew, delta[k], xr, scr, W, bc)
                p.wait("sync", t_n)
                t_st = p.st(d["x2"][k * 128:(k + 1) * 128, c0:c0 + W], xnew[:])
                p.wait("vector", t_st)
            p.barrier()



def emit_s1(p, g):
    d = g.d
    nc = p.nc
    W = 1024
    NH = 16
    with ExitStack() as es:
        xln = [es.enter_context(nc.sbuf_tensor(f"s1_xln{k}", [128, W], BF16))
               for k in range(NKC)]
        sqb = es.enter_context(nc.sbuf_tensor("s1_sq", [128, W], BF16))
        scr = es.enter_context(nc.sbuf_tensor("s1_scr", [128, 512], F32))
        y = [es.enter_context(nc.sbuf_tensor(f"s1_y{h}", [128, W], BF16))
             for h in range(NH)]
        delta = [es.enter_context(nc.sbuf_tensor(f"s1_d{k}", [128, W], BF16))
                 for k in range(NKC)]
        qf = es.enter_context(nc.sbuf_tensor("s1_qf", [128, W], F32))
        kf = es.enter_context(nc.sbuf_tensor("s1_kf", [128, W], F32))
        vf = es.enter_context(nc.sbuf_tensor("s1_vf", [128, W], F32))
        af = es.enter_context(nc.sbuf_tensor("s1_af", [128, W], F32))
        gf = es.enter_context(nc.sbuf_tensor("s1_gf", [128, W], F32))
        w1 = es.enter_context(nc.sbuf_tensor("s1_w1", [128, W], F32))
        w2 = es.enter_context(nc.sbuf_tensor("s1_w2", [128, W], F32))
        w3 = es.enter_context(nc.sbuf_tensor("s1_w3", [128, W], F32))
        wb = [es.enter_context(nc.sbuf_tensor(f"s1_wb{i}", [128, 2048], BF16))
              for i in range(2)]
        wo = [es.enter_context(nc.sbuf_tensor(f"s1_wo{i}", [128, 2048], BF16))
              for i in range(2)]
        stc = es.enter_context(nc.sbuf_tensor("s1_stc", [128, NH], F32))
        xres = [es.enter_context(nc.sbuf_tensor(f"s1_xr{i}", [128, W], F32)) for i in range(2)]
        xnew = es.enter_context(nc.sbuf_tensor("s1_xn", [128, W], F32))
        p.barrier()

        for tt in range(T // W):
            c0 = tt * W
            toks = [p.ca(xln[k][:], d["x0"][k * 128:(k + 1) * 128, c0:c0 + W])
                    for k in range(NKC)]
            st_tok = None
            for k in range(NKC):
                st_tok = ln_stats_chunk(p, g, xln[k], sqb, W, k == 0, k == NKC - 1,
                                        src_tok=toks[k], war=st_tok)
            bc = ln_rows_and_bcast(p, g, W, D, st_tok)
            for k in range(NKC):
                ln_norm_chunk(p, g, xln[k], xln[k], scr, W, bc)
            p.barrier()

            wtoks = {}
            evg = {}
            rows_guard = None
            st45_guard = None
            bc67_guard = None
            prev_ty = None
            for h in range(NH):
                evac = {}
                if prev_ty is not None:
                    p.wait("scalar", prev_ty)
                for isec, sec in enumerate(("q", "k", "v", "a", "g")):
                    gi = h * 5 + isec
                    bsel = gi % 2
                    if gi >= 2:
                        p.wait("gpsimd", wtoks[gi - 2])
                    w_t = p.ca(wb[bsel][:].rearrange("p (c o) -> p c o", o=128),
                               w_rearr(d["w0_in"], isec * 2048 + h * 128, 128))
                    pb = [g.ps[2 * bsel + j] for j in range(2)]
                    if gi >= 2:
                        p.wait("tensor", evg[gi - 2])
                    p.wait("tensor", w_t)
                    toks_m = []
                    for s in range(2):
                        toks_m.append(mm_chain(p, pb[s][:, :],
                                               [(wb[bsel][:, kc * 128:(kc + 1) * 128],
                                                 xln[kc][:, s * 512:(s + 1) * 512])
                                                for kc in range(NKC)]))
                    wtoks[gi] = toks_m[1]
                    dst = {"q": qf, "k": kf, "v": vf, "a": af, "g": gf}[sec]
                    t_e = None
                    for s in range(2):
                        p.wait("scalar", toks_m[s])
                        if sec == "k":
                            t_e = ACT(p, dst[:, s * 512:(s + 1) * 512], pb[s][:, :], AF.Tanh)
                        elif sec == "a":
                            t_e = ACT(p, dst[:, s * 512:(s + 1) * 512], pb[s][:, :],
                                      AF.Sigmoid, bias=2.0)
                        elif sec == "g":
                            t_e = ACT(p, dst[:, s * 512:(s + 1) * 512], pb[s][:, :], AF.Sigmoid)
                        else:
                            t_e = ACT(p, dst[:, s * 512:(s + 1) * 512], pb[s][:, :], AF.Copy)
                    evac[sec] = t_e
                    evg[gi] = t_e
                p.wait("vector", evac["a"])
                t_a = TS(p, af[:], af[:], 0.6, 0.9995, ALU.max, ALU.min)
                p.wait("vector", evac["k"])
                p.wait("vector", evac["v"])
                TT(p, w1[:], kf[:], vf[:], ALU.mult)
                TT(p, w2[:], af[:], g.m0_64[:, 0:W], ALU.mult)
                TT(p, w3[:], af[:], g.m1_64[:, 0:W], ALU.mult)
                SCAN(p, w2[:], w2[:], w3[:], 1.0, ALU.mult, ALU.add)
                TS(p, w3[:], w2[:], 1e6, 1.0, ALU.mult, ALU.min)
                TT(p, w1[:], w1[:], w3[:], ALU.mult)
                init = 0.0 if tt == 0 else stc[:, h:h + 1]
                t_sc = SCAN(p, w2[:], af[:], w1[:], init, ALU.mult, ALU.add)
                CP(p, stc[:, h:h + 1], w2[:, W - 1:W])
                p.wait("tensor", evac["q"])
                if st45_guard is not None:
                    p.wait("tensor", st45_guard)
                qs_toks = []
                for s in range(2):
                    if s == 1:
                        p.wait("scalar", qs_toks[0])
                    t_sq2 = p.ac(lambda e, o=sqb[:, 0:512], i=qf[:, s * 512:(s + 1) * 512]:
                                 e.activation(o, i, AF.Square))
                    p.wait("tensor", t_sq2)
                    qs_toks.append(MM(p, g.ps[4 + s][0:1, :], g.onesb[:, 0:1],
                                      sqb[:, 0:512], True, True))
                if rows_guard is not None:
                    p.wait("vector", rows_guard)
                qr0 = g.rows[0:1, 0:512]
                qr1 = g.rows[0:1, 512:1024]
                p.wait("vector", qs_toks[0])
                p.wait("vector", qs_toks[1])
                TS(p, qr0, g.ps[4][0:1, :], 1.0 / 128, None, ALU.mult)
                t_r = TS(p, qr1, g.ps[5][0:1, :], 1.0 / 128, None, ALU.mult)
                p.wait("scalar", t_r)
                ACT(p, qr0, qr0, AF.Ln, bias=1.1920928955078125e-07)
                ACT(p, qr1, qr1, AF.Ln, bias=1.1920928955078125e-07)
                ACT(p, qr0, qr0, AF.Exp, scale=-0.5)
                t_r2 = ACT(p, qr1, qr1, AF.Exp, scale=-0.5)
                if bc67_guard is not None:
                    p.wait("tensor", bc67_guard)
                p.wait("tensor", t_r2)
                MM(p, g.ps[6][:, :], g.ones[0:1, :], qr0, True, True, inc=False)
                t_bc = MM(p, g.ps[7][:, :], g.ones[0:1, :], qr1, True, True)
                rows_guard = t_bc
                p.wait("vector", t_bc)
                t1a = TT(p, w1[:, 0:512], qf[:, 0:512], g.ps[6][:, :], ALU.mult)
                t1b = TT(p, w1[:, 512:W], qf[:, 512:W], g.ps[7][:, :], ALU.mult)
                st45_guard = t1b
                bc67_guard = t1b
                TT(p, w1[:], w1[:], w2[:], ALU.mult)
                TT(p, w1[:], w1[:], vf[:], ALU.subtract)
                TT(p, w1[:], w1[:], gf[:], ALU.mult)
                t_y = TT(p, y[h][:], w1[:], vf[:], ALU.add)
                prev_ty = t_y
            p.barrier()

            if DEBUG_TAP == "s1_state":
                p.st(d["t_dbg"].bitcast(F32)[0:128, tt * 16:(tt + 1) * 16], stc[:])
                p.barrier()

            dn_toks = {}
            prev_mm = None
            for oc in range(NKC):
                bsel = oc % 2
                if oc >= 2:
                    p.wait("gpsimd", prev_mm)
                wo_t = p.ca(wo[bsel][:].rearrange("p (c o) -> p c o", o=128),
                            w_rearr(d["w0_out"], oc * 128, 128))
                bks = [g.ps[2 * bsel + j] for j in range(2)]
                if oc >= 2:
                    p.wait("tensor", dn_toks[oc - 2])
                p.wait("tensor", wo_t)
                toks_d = []
                for s in range(2):
                    toks_d.append(mm_chain(p, bks[s][:, :],
                                           [(wo[bsel][:, kc * 128:(kc + 1) * 128],
                                             y[kc][:, s * 512:(s + 1) * 512])
                                            for kc in range(NKC)]))
                prev_mm = toks_d[1]
                t_ev = None
                for s in range(2):
                    p.wait("scalar", toks_d[s])
                    t_ev = ACT(p, delta[oc][:, s * 512:(s + 1) * 512], bks[s][:, :], AF.Copy)
                dn_toks[oc] = t_ev
            p.barrier()

            st_tok = None
            for k in range(NKC):
                st_tok = ln_stats_chunk(p, g, delta[k], sqb, W, k == 0, k == NKC - 1,
                                        war=st_tok)
            bc = ln_rows_and_bcast(p, g, W, D, st_tok)
            for k in range(NKC):
                xr = xres[k % 2]
                t_x = p.ld(xr[:], d["x0"][k * 128:(k + 1) * 128, c0:c0 + W])
                p.wait("vector", t_x)
                t_n = ln_resid_chunk(p, g, xnew, delta[k], xr, scr, W, bc)
                p.wait("sync", t_n)
                t_st = p.st(d["x1"][k * 128:(k + 1) * 128, c0:c0 + W], xnew[:])
                p.wait("vector", t_st)
            p.barrier()



def emit_s3(p, g):
    d = g.d
    nc = p.nc
    W = 1024
    with ExitStack() as es:
        xln = [es.enter_context(nc.sbuf_tensor(f"s3_xln{k}", [128, W], BF16))
               for k in range(NKC)]
        sqb = es.enter_context(nc.sbuf_tensor("s3_sq", [128, W], BF16))
        scr = es.enter_context(nc.sbuf_tensor("s3_scr", [128, 512], F32))
        wz = [es.enter_context(nc.sbuf_tensor(f"s3_wz{i}", [128, NKC * 512], BF16))
              for i in range(2)]
        wx = [es.enter_context(nc.sbuf_tensor(f"s3_wx{i}", [128, 2048], BF16))
              for i in range(2)]
        wdt = es.enter_context(nc.sbuf_tensor("s3_wdt", [128, NKC * 64], BF16))
        zst = [es.enter_context(nc.sbuf_tensor(f"s3_zst{i}", [128, 512], BF16))
               for i in range(2)]
        xst = [es.enter_context(nc.sbuf_tensor(f"s3_xst{i}", [128, W], BF16))
               for i in range(2)]
        dst = [es.enter_context(nc.sbuf_tensor(f"s3_dst{i}", [64, W], F32))
               for i in range(2)]
        zzt = es.enter_context(nc.sbuf_tensor("s3_zzt", [128, 4], BF16))
        p.barrier()
        t_z = p.dv(lambda e, o=zzt[:]: e.memset(o, 0.0))
        p.wait("sync", t_z)
        for oc in range(48):
            p.st(d["xbc"][oc * 128:(oc + 1) * 128, 0:3], zzt[:, 0:3])
        p.barrier()

        for tt in range(T // W):
            c0 = tt * W
            toks = [p.ca(xln[k][:], d["x2"][k * 128:(k + 1) * 128, c0:c0 + W])
                    for k in range(NKC)]
            st_tok = None
            for k in range(NKC):
                st_tok = ln_stats_chunk(p, g, xln[k], sqb, W, k == 0, k == NKC - 1,
                                        src_tok=toks[k], war=st_tok)
            bc = ln_rows_and_bcast(p, g, W, D, st_tok)
            for k in range(NKC):
                ln_norm_chunk(p, g, xln[k], xln[k], scr, W, bc)
            p.barrier()

            ev = {}
            wtok_prev = None
            st_prev = {}
            for zc in range(8):
                bsel = zc % 2
                if zc >= 2:
                    p.wait("gpsimd", wtok_prev)
                w_t = p.ca(wz[bsel][:].rearrange("p (c o) -> p c o", o=512),
                           w_rearr(d["w1_in"], zc * 512, 512))
                for tch in range(8):
                    bk = g.ps[tch]
                    if zc >= 1:
                        p.wait("tensor", ev[tch])
                    if tch == 0:
                        p.wait("tensor", w_t)
                    t_mm = mm_chain(p, bk[:, :],
                                    [(xln[kc][:, tch * 128:(tch + 1) * 128],
                                      wz[bsel][:, kc * 512:(kc + 1) * 512])
                                     for kc in range(NKC)])
                    if tch == 7:
                        wtok_prev = t_mm
                    sb_ = zst[tch % 2]
                    p.wait("scalar", t_mm)
                    p.wait("scalar", st_prev.get(tch % 2))
                    t_e = ACT(p, sb_[:], bk[:, :], AF.Silu)
                    ev[tch] = t_e
                    p.wait("sync", t_e)
                    st_prev[tch % 2] = p.st(
                        d["zs"][c0 + tch * 128:c0 + (tch + 1) * 128,
                                zc * 512:(zc + 1) * 512], sb_[:])
            p.barrier()

            dn = {}
            prev_mm = None
            st_prev = {}
            for oc in range(48):
                bsel = oc % 2
                if oc >= 2:
                    p.wait("gpsimd", prev_mm)
                w_t = p.ca(wx[bsel][:].rearrange("p (c o) -> p c o", o=128),
                           w_rearr(d["w1_in"], 4096 + oc * 128, 128))
                bks = [g.ps[2 * bsel + j] for j in range(2)]
                if oc >= 2:
                    p.wait("tensor", dn[oc - 2])
                p.wait("tensor", w_t)
                toks_d = []
                for sx in range(2):
                    toks_d.append(mm_chain(p, bks[sx][:, :],
                                           [(wx[bsel][:, kc * 128:(kc + 1) * 128],
                                             xln[kc][:, sx * 512:(sx + 1) * 512])
                                            for kc in range(NKC)]))
                prev_mm = toks_d[1]
                sb_ = xst[oc % 2]
                t_e = None
                for sx in range(2):
                    p.wait("scalar", toks_d[sx])
                    p.wait("scalar", st_prev.get(oc % 2))
                    t_e = ACT(p, sb_[:, sx * 512:(sx + 1) * 512], bks[sx][:, :], AF.Copy)
                dn[oc] = t_e
                p.wait("sync", t_e)
                st_prev[oc % 2] = p.st(
                    d["xbc"][oc * 128:(oc + 1) * 128, 3 + c0:3 + c0 + W], sb_[:])
            p.barrier()

            p.barrier()
            w_t = p.ca(wdt[:].rearrange("p (c o) -> p c o", o=64),
                       w_rearr(d["w1_in"], 10240, 64))
            p.wait("tensor", w_t)
            for sx in range(2):
                bk = g.ps[sx]
                t_mm = mm_chain(p, bk[0:64, :],
                                [(wdt[:, kc * 64:(kc + 1) * 64],
                                  xln[kc][:, sx * 512:(sx + 1) * 512])
                                 for kc in range(NKC)])
                p.wait("scalar", t_mm)
                t_e = ACT(p, dst[0][0:64, sx * 512:(sx + 1) * 512], bk[0:64, :], AF.Copy)
            p.wait("sync", t_e)
            p.st(d["dtr"][0:64, c0:c0 + W], dst[0][0:64, :])
            p.barrier()



def emit_s5a(p, g):
    d = g.d
    nc = p.nc
    W = 1024
    with ExitStack() as es:
        xln = [es.enter_context(nc.sbuf_tensor(f"s5_xln{k}", [128, W], BF16))
               for k in range(NKC)]
        sqb = es.enter_context(nc.sbuf_tensor("s5_sq", [128, W], BF16))
        scr = es.enter_context(nc.sbuf_tensor("s5_scr", [128, 512], F32))
        hg = es.enter_context(nc.sbuf_tensor("s5_hg", [128, 512], F32))
        wgr = es.enter_context(nc.sbuf_tensor("s5_wgr", [128, NKC * 8], BF16))
        bgr = es.enter_context(nc.sbuf_tensor("s5_bgr", [1, 8], F32))
        bgbc = es.enter_context(nc.sbuf_tensor("s5_bgbc", [128, 8], F32))
        sel = es.enter_context(nc.sbuf_tensor("s5_sel", [8, 1024], F32))
        wfm_sb = es.enter_context(nc.sbuf_tensor("s5_wfm", [8, W], F32))
        wbc = es.enter_context(nc.sbuf_tensor("s5_wbc", [128, W], F32))
        wr = [es.enter_context(nc.sbuf_tensor(f"s5_wr{i}", [128, 8], F32))
              for i in range(4)]
        cc = es.enter_context(nc.sbuf_tensor("s5_cc", [128, 8], F32))
        wg2 = [es.enter_context(nc.sbuf_tensor(f"s5_wg{i}", [128, 2048], BF16))
               for i in range(2)]
        wu2 = [es.enter_context(nc.sbuf_tensor(f"s5_wu{i}", [128, 2048], BF16))
               for i in range(2)]
        bgc = es.enter_context(nc.sbuf_tensor("s5_bgc", [128, 48], F32))
        buc = es.enter_context(nc.sbuf_tensor("s5_buc", [128, 48], F32))
        bsuc = es.enter_context(nc.sbuf_tensor("s5_bsuc", [128, 16], F32))
        hst = [es.enter_context(nc.sbuf_tensor(f"s5_hst{i}", [128, W], BF16))
               for i in range(2)]
        p.ld(bgr[:], d["w2_bg"][:])
        p.ld(sel[:], d["c_sel"][:])
        p.ld(bgc[:], d["w2_bgate"][:])
        p.ld(buc[:], d["w2_bup"][:])
        p.ld(bsuc[:], d["w2_bsu"][:])
        t_w = p.ca(wgr[:].rearrange("p (c o) -> p c o", o=8),
                   w_rearr(d["w2_wg"], 0, 8))
        p.barrier()
        t_mm = MM(p, g.ps[0][:, 0:8], g.ones[0:1, :], bgr[0:1, :], True, True)
        p.wait("vector", t_mm)
        t_cp = CP(p, bgbc[:], g.ps[0][:, 0:8])
        p.barrier()

        for tt in range(T // W):
            c0 = tt * W
            toks = [p.ca(xln[k][:], d["x3"][k * 128:(k + 1) * 128, c0:c0 + W])
                    for k in range(NKC)]
            st_tok = None
            for k in range(NKC):
                st_tok = ln_stats_chunk(p, g, xln[k], sqb, W, k == 0, k == NKC - 1,
                                        src_tok=toks[k], war=st_tok)
            bc = ln_rows_and_bcast(p, g, W, D, st_tok)
            for k in range(NKC):
                ln_norm_chunk(p, g, xln[k], xln[k], scr, W, bc)
            p.barrier()

            do_router = "router" in S5A_PHASES
            tr_guard = None
            rt_guard = {}
            wr_guard = None
            for tch in range(8 if do_router else 0):
                bk = g.ps[tch % 2]
                if tch >= 2:
                    p.wait("tensor", rt_guard[tch - 2])
                t_mm = mm_chain(p, bk[:, 0:8],
                                [(xln[kc][:, tch * 128:(tch + 1) * 128],
                                  wgr[:, kc * 8:(kc + 1) * 8]) for kc in range(NKC)])
                p.wait("vector", t_mm)
                if wr_guard is not None:
                    p.wait("vector", wr_guard)
                t_tt0 = TT(p, wr[0][:], bk[:, 0:8], bgbc[:], ALU.add)
                rt_guard[tch] = t_tt0
                RED(p, cc[:, 0:1], wr[0][:], ALU.min)
                TS(p, wr[1][:], wr[0][:], cc[:, 0:1], None, ALU.is_equal)
                STT(p, wr[2][:], wr[1][:], 1e6, wr[0][:], ALU.mult, ALU.add)
                RED(p, cc[:, 1:2], wr[2][:], ALU.min)
                TS(p, wr[1][:], wr[0][:], cc[:, 1:2], None, ALU.is_gt)
                RED(p, cc[:, 2:3], wr[0][:], ALU.max)
                t_n = TS(p, cc[:, 3:4], cc[:, 2:3], -1.0, None, ALU.mult)
                p.wait("scalar", t_n)
                t_ex = p.ac(lambda e, o=wr[2][:], i=wr[0][:], b=cc[:, 3:4]:
                            e.activation(o, i, AF.Exp, bias=b))
                p.wait("vector", t_ex)
                TT(p, wr[3][:], wr[2][:], wr[1][:], ALU.mult)
                RED(p, cc[:, 4:5], wr[3][:], ALU.add)
                t_rz = p.dv(lambda e, o=cc[:, 5:6], i=cc[:, 4:5]: e.reciprocal(o, i))
                p.wait("vector", t_rz)
                t_wtm = TS(p, wr[0][:], wr[3][:], cc[:, 5:6], None, ALU.mult)
                if tr_guard is not None:
                    p.wait("tensor", tr_guard)
                p.wait("tensor", t_wtm)
                t_tr = TRN(p, g.ps[7][0:8, 0:128], wr[0][:], g.idn[:])
                wr_guard = t_tr
                p.wait("vector", t_tr)
                tr_guard = CP(p, wfm_sb[0:8, tch * 128:(tch + 1) * 128],
                              g.ps[7][0:8, 0:128])
            if do_router:
                p.wait("sync", tr_guard)
                p.st(d["wfm"][:, c0:c0 + W], wfm_sb[:])
            p.barrier()

            ev = {}
            ev_mm = {}
            hst_guard = {}
            gi = 0
            for e_ in range(6 if "experts" in S5A_PHASES else 0):
                if gi >= 1:
                    p.wait("tensor", ev[gi - 1])
                if gi >= 2:
                    p.wait("tensor", ev[gi - 2])
                t_bc = None
                for sx in range(2):
                    if t_bc is not None:
                        p.wait("tensor", t_bc)
                    t_mm = MM(p, g.ps[6][:, :], sel[0:8, e_ * 128:(e_ + 1) * 128],
                              wfm_sb[0:8, sx * 512:(sx + 1) * 512], True, True)
                    p.wait("vector", t_mm)
                    t_bc = CP(p, wbc[:, sx * 512:(sx + 1) * 512], g.ps[6][:, :])
                prev_mm = None
                for ic in range(8):
                    bsel = ic % 2
                    if gi >= 2:
                        p.wait("gpsimd", ev_mm[gi - 2])
                    wg_t = p.ca(wg2[bsel][:].rearrange("p (c o) -> p c o", o=128),
                                w_rearr(d["w2_wgate"][e_ * 2048:(e_ + 1) * 2048, :],
                                        ic * 128, 128))
                    wu_t = p.ca(wu2[bsel][:].rearrange("p (c o) -> p c o", o=128),
                                w_rearr(d["w2_wup"][e_ * 2048:(e_ + 1) * 2048, :],
                                        ic * 128, 128))
                    bks = [g.ps[4 * bsel + j] for j in range(4)]
                    if gi >= 2:
                        p.wait("tensor", ev[gi - 2])
                    if bsel == 1:
                        p.wait("tensor", t_bc)
                    p.wait("tensor", wg_t)
                    p.wait("tensor", wu_t)
                    tok_u = None
                    for sx in range(2):
                        mm_chain(p, bks[sx][:, :],
                                 [(wg2[bsel][:, kc * 128:(kc + 1) * 128],
                                   xln[kc][:, sx * 512:(sx + 1) * 512])
                                  for kc in range(NKC)], inc=False)
                        tok_u = mm_chain(p, bks[2 + sx][:, :],
                                         [(wu2[bsel][:, kc * 128:(kc + 1) * 128],
                                           xln[kc][:, sx * 512:(sx + 1) * 512])
                                          for kc in range(NKC)])
                    last = ev.get(gi - 1)
                    t_e = None
                    for sx in range(2):
                        p.wait("scalar", tok_u)
                        if t_e is not None or last is not None:
                            p.wait("scalar", t_e if t_e is not None else last)
                        t_hg = p.ac(lambda e, o=hg[:], i=bks[sx][:, :],
                                    b=bgc[:, e_ * 8 + ic:e_ * 8 + ic + 1]:
                                    e.activation(o, i, AF.Silu, bias=b))
                        p.wait("vector", t_hg)
                        p.wait("vector", t_bc)
                        p.wait("vector", hst_guard.get(ic % 2))
                        STT(p, scr[:], bks[2 + sx][:, :],
                            buc[:, e_ * 8 + ic:e_ * 8 + ic + 1], hg[:], ALU.add, ALU.mult)
                        t_e = TT(p, hst[ic % 2][:, sx * 512:(sx + 1) * 512], scr[:],
                                 wbc[:, sx * 512:(sx + 1) * 512], ALU.mult)
                    ev[gi] = t_e
                    ev_mm[gi] = tok_u
                    p.wait("sync", t_e)
                    hst_guard[ic % 2] = p.st(
                        d["hxp"][e_ * 1024 + ic * 128:e_ * 1024 + (ic + 1) * 128,
                                 c0:c0 + W], hst[ic % 2][:])
                    gi += 1
            p.barrier()

            ev = {}
            sh_guard = {}
            prev_mm = None
            for ic in range(16 if "shared" in S5A_PHASES else 0):
                bsel = ic % 2
                if ic >= 2:
                    p.wait("gpsimd", prev_mm)
                wg_t = p.ca(wg2[bsel][:].rearrange("p (c o) -> p c o", o=128),
                            w_rearr(d["w2_wsu"], ic * 128, 128))
                bks = [g.ps[2 * bsel + j] for j in range(2)]
                if ic >= 2:
                    p.wait("tensor", ev[ic - 2])
                p.wait("tensor", wg_t)
                toks_d = []
                for sx in range(2):
                    toks_d.append(mm_chain(p, bks[sx][:, :],
                                           [(wg2[bsel][:, kc * 128:(kc + 1) * 128],
                                             xln[kc][:, sx * 512:(sx + 1) * 512])
                                            for kc in range(NKC)]))
                prev_mm = toks_d[1]
                t_e = None
                for sx in range(2):
                    p.wait("scalar", toks_d[sx])
                    p.wait("scalar", ev.get(ic - 1))
                    p.wait("scalar", sh_guard.get(ic % 2))
                    t_e = ACT(p, hst[ic % 2][:, sx * 512:(sx + 1) * 512], bks[sx][:, :],
                              AF.Silu, bias=bsuc[:, ic:ic + 1])
                ev[ic] = t_e
                p.wait("sync", t_e)
                sh_guard[ic % 2] = p.st(
                    d["hxp"][6144 + ic * 128:6144 + (ic + 1) * 128, c0:c0 + W],
                    hst[ic % 2][:])
            p.barrier()



def emit_s5b(p, g):
    d = g.d
    nc = p.nc
    W = 512
    with ExitStack() as es:
        hsb = es.enter_context(nc.sbuf_tensor("s5b_h", [128, 64 * W], BF16))
        wdb = [es.enter_context(nc.sbuf_tensor(f"s5b_wd{i}", [128, 64 * 128], BF16))
               for i in range(2)]
        delta = [es.enter_context(nc.sbuf_tensor(f"s5b_d{k}", [128, W], BF16))
                 for k in range(NKC)]
        wfm5 = es.enter_context(nc.sbuf_tensor("s5b_wfm", [8, W], F32))
        bdt = es.enter_context(nc.sbuf_tensor("s5b_bdt", [6, D], F32))
        bsdc = es.enter_context(nc.sbuf_tensor("s5b_bsdc", [128, 16], F32))
        sqb = es.enter_context(nc.sbuf_tensor("s5b_sq", [128, W], BF16))
        scr = es.enter_context(nc.sbuf_tensor("s5b_scr", [128, 512], F32))
        xres = [es.enter_context(nc.sbuf_tensor(f"s5b_xr{i}", [128, W], F32))
                for i in range(2)]
        xnew = es.enter_context(nc.sbuf_tensor("s5b_xn", [128, W], F32))
        p.ld(bdt[:], d["w2_bdown"][:])
        p.ld(bsdc[:], d["w2_bsd"][:])
        p.barrier()

        for tt in range(T // W):
            c0 = tt * W
            t_h = p.ld(hsb[:].rearrange("p (r t) -> p r t", t=W),
                       d["hxp"][:, c0:c0 + W].rearrange("(r p) t -> p r t", p=128))
            t_w5 = p.ld(wfm5[:], d["wfm"][:, c0:c0 + W])
            dn = {}
            prev_mm = None
            for oc in range(NKC):
                bsel = oc % 2
                if oc >= 2:
                    p.wait("gpsimd", prev_mm)
                w_t = p.ca(wdb[bsel][:, 0:48 * 128].rearrange("p (c o) -> p c o", o=128),
                           w_rearr(d["w2_wdown"], oc * 128, 128))
                w_t2 = p.ca(wdb[bsel][:, 48 * 128:64 * 128].rearrange("p (c o) -> p c o", o=128),
                            w_rearr(d["w2_wsd"], oc * 128, 128))
                bk = g.ps[oc % 4]
                if oc >= 4:
                    p.wait("tensor", dn[oc - 4])
                p.wait("tensor", w_t)
                p.wait("tensor", w_t2)
                if oc == 0:
                    p.wait("tensor", t_h)
                    p.wait("tensor", t_w5)
                pairs = [(wdb[bsel][:, c * 128:(c + 1) * 128],
                          hsb[:, c * W:(c + 1) * W]) for c in range(64)]
                for i, (l, r) in enumerate(pairs):
                    MM(p, bk[:, :], l, r, i == 0, False, inc=False)
                t_mm = MM(p, bk[:, :], bdt[0:6, oc * 128:(oc + 1) * 128],
                          wfm5[0:6, :], False, True)
                prev_mm = t_mm
                p.wait("vector", t_mm)
                t_e = TS(p, delta[oc][:], bk[:, :], bsdc[:, oc:oc + 1], None, ALU.add)
                dn[oc] = t_e
            p.barrier()

            st_tok = None
            for k in range(NKC):
                st_tok = ln_stats_chunk(p, g, delta[k], sqb, W, k == 0, k == NKC - 1,
                                        war=st_tok)
            bc = ln_rows_and_bcast(p, g, W, D, st_tok)
            for k in range(NKC):
                xr = xres[k % 2]
                t_x = p.ld(xr[:], d["x3"][k * 128:(k + 1) * 128, c0:c0 + W])
                p.wait("vector", t_x)
                t_n = ln_resid_chunk(p, g, xnew, delta[k], xr, scr, W, bc)
                p.wait("sync", t_n)
                t_st = p.st(d["out"][k * 128:(k + 1) * 128, c0:c0 + W], xnew[:])
                p.wait("vector", t_st)
            p.barrier()



def emit_s4(p, g):
    d = g.d
    nc = p.nc
    W = 512
    NCH = W // 128
    with ExitStack() as es:
        sb = lambda n, shp, dt=F32: es.enter_context(nc.sbuf_tensor(n, shp, dt))
        x_sb = sb("s4_x", [128, 8 * W], BF16)
        b_sb = sb("s4_b", [128, 8 * W], BF16)
        c_sb = sb("s4_c", [128, 32 * W], BF16)
        xin = [sb(f"s4_xin{i}", [128, W + 3], BF16) for i in range(2)]
        cv = [sb(f"s4_cv{i}", [128, W]) for i in range(3)]
        cwt = sb("s4_cw", [128, 192])
        cbt = sb("s4_cb", [128, 48])
        dtb = sb("s4_dtb", [64, 1])
        alog = sb("s4_alog", [64, 1])
        nA = sb("s4_nA", [64, 1])
        d64 = sb("s4_d64", [1, 64])
        dcol = sb("s4_dcol", [128, 64])
        dtv = sb("s4_dtv", [64, W])
        lndt = sb("s4_lndt", [64, W])
        dAf = sb("s4_dA", [64, W])
        cum = sb("s4_cum", [64, W])
        rvc = sb("s4_rvc", [64, W])
        d2e = sb("s4_d2e", [64, W])
        eend = sb("s4_eend", [64, NCH])
        erow = sb("s4_erow", [1, 64])
        hdecp = sb("s4_hdecp", [128, 32])
        tmA = sb("s4_tmA", [128, 64])
        tmL = sb("s4_tmL", [128, 64])
        tmE = sb("s4_tmE", [128, 64])
        tmD = sb("s4_tmD", [128, 64])
        x_tm = sb("s4_xtm", [128, 16 * 64], BF16)
        b_tm = sb("s4_btm", [128, 16 * 64], BF16)
        bdup = sb("s4_bdup", [128, 16 * 128], BF16)
        Ht = sb("s4_H", [128, 2048])
        hb16 = sb("s4_hb16", [128, 128], BF16)
        lda4 = sb("s4_lda4", [128, 512])
        mexp4 = sb("s4_mexp4", [128, 512], BF16)
        gma = sb("s4_gma", [128, 256], BF16)
        gmb = sb("s4_gmb", [128, 256], BF16)
        bd4 = sb("s4_bd4", [128, 256], BF16)
        w1t = sb("s4_w1t", [128, 256])
        w2t = sb("s4_w2t", [128, 256])
        w3a = sb("s4_w3a", [128, 128])
        w3b = sb("s4_w3b", [128, 128])
        hmt = sb("s4_hmt", [128, 128])
        Yt = sb("s4_Y", [128, 4096], BF16)
        ssq = sb("s4_ssq", [128, 64])
        rstd = sb("s4_rstd", [128, 64])
        zt = [sb("s4_z0", [128, 4096], BF16)]
        nrm = sb("s4_nrm", [128, 32 * W], BF16)
        wob = [sb(f"s4_wob{i}", [128, 32 * 128], BF16) for i in range(2)]
        wtmp = wob[0][:, 0:2048].rearrange("p n -> p n") if False else None
        nwsb = sb("s4_nw", [128, 32])
        delta = [x_sb[:, k * W:(k + 1) * W] for k in range(8)] + \
                [b_sb[:, (k - 8) * W:(k - 7) * W] for k in range(8, 16)]
        sqb = sb("s4_sqb", [128, W], BF16)
        scr = sb("s4_scr", [128, 512])
        xres = [sb(f"s4_xr{i}", [128, W]) for i in range(2)]
        xnew = sb("s4_xn", [128, W])
        mtop = sb("s4_mtop", [1, 128])
        mbot = sb("s4_mbot", [1, 128])

        p.ld(cwt[:], d["w1_cw"][:])
        p.ld(cbt[:], d["w1_cb"][:])
        p.ld(dtb[:], d["w1_dtb"][:])
        p.ld(alog[:], d["w1_alog"][:])
        p.ld(d64[:], d["w1_d64"][:])
        p.ld(nwsb[:], d["w1_nw"][:])
        p.ld(mtop[:], d["c_mtop"][:])
        t_pl = p.ld(mbot[:], d["c_mbot"][:])
        p.barrier()
        t_a = ACT(p, nA[:], alog[:], AF.Exp)
        p.wait("vector", t_a)
        TS(p, nA[:], nA[:], -1.0, None, ALU.mult)
        t_mm = MM(p, g.ps[0][:, 0:64], g.ones[0:1, :], d64[0:1, :], True, True)
        p.wait("vector", t_mm)
        CP(p, dcol[:], g.ps[0][:, 0:64])
        p.dv(lambda e, o=Ht[:]: e.memset(o, 0.0))
        pw_g = {}
        for kc in range(32):
            if kc >= 1:
                p.wait("gpsimd", pw_g[kc - 1])
            t_w = p.ca(wob[0][:, 0:2048], d["w1_out"][kc * 128:(kc + 1) * 128, :])
            p.wait("vector", t_w)
            t_s = TS(p, wob[0][:, 0:2048], wob[0][:, 0:2048],
                     nwsb[:, kc:kc + 1], None, ALU.mult)
            p.wait("sync", t_s)
            pw_g[kc] = p.st(d["wout_bf"][kc * 128:(kc + 1) * 128, :], wob[0][:, 0:2048])
        p.barrier()

        zld_g = {}
        grp_dv = None
        grp_ac = None
        ch_dv = None
        ch_ac = None
        for tt in range(T // W):
            c0 = tt * W
            cv_g = None
            t_si = None
            for oc in range(48):
                bsel = oc % 2
                if cv_g is not None:
                    p.wait("sync", cv_g)
                t_x = p.ld(xin[bsel][:], d["xbc"][oc * 128:(oc + 1) * 128,
                                                  c0:c0 + W + 3])
                p.wait("vector", t_x)
                if t_si is not None:
                    p.wait("vector", t_si)
                TS(p, cv[0][:], xin[bsel][:, 0:W], cwt[:, oc * 4:oc * 4 + 1],
                   None, ALU.mult)
                STT(p, cv[1][:], xin[bsel][:, 1:W + 1], cwt[:, oc * 4 + 1:oc * 4 + 2],
                    cv[0][:], ALU.mult, ALU.add)
                STT(p, cv[2][:], xin[bsel][:, 2:W + 2], cwt[:, oc * 4 + 2:oc * 4 + 3],
                    cv[1][:], ALU.mult, ALU.add)
                t_c = STT(p, cv[0][:], xin[bsel][:, 3:W + 3], cwt[:, oc * 4 + 3:oc * 4 + 4],
                          cv[2][:], ALU.mult, ALU.add)
                cv_g = t_c
                p.wait("scalar", t_c)
                if oc < 8:
                    dst = x_sb[:, oc * W:(oc + 1) * W]
                elif oc < 16:
                    dst = b_sb[:, (oc - 8) * W:(oc - 7) * W]
                else:
                    dst = c_sb[:, (oc - 16) * W:(oc - 15) * W]
                t_si = ACT(p, dst, cv[0][:], AF.Silu, bias=cbt[:, oc:oc + 1])
            t_d = p.ld(dtv[0:64, :], d["dtr"][0:64, c0:c0 + W])
            p.wait("vector", t_d)
            p.wait("vector", t_si)
            TS(p, cv[0][0:64, :], dtv[0:64, :], dtb[:, 0:1], None, ALU.add)
            t_rl = TS(p, cv[1][0:64, :], cv[0][0:64, :], 0.0, None, ALU.max)
            p.wait("scalar", t_rl)
            ACT(p, cv[2][0:64, :], cv[0][0:64, :], AF.Abs)
            t_e1 = ACT(p, cv[2][0:64, :], cv[2][0:64, :], AF.Exp, scale=-1.0)
            t_e2 = ACT(p, cv[2][0:64, :], cv[2][0:64, :], AF.Ln, bias=1.0)
            p.wait("vector", t_e2)
            t_dtv = TT(p, dtv[0:64, :], cv[1][0:64, :], cv[2][0:64, :], ALU.add)
            p.wait("scalar", t_dtv)
            t_ln = ACT(p, lndt[0:64, :], dtv[0:64, :], AF.Ln)
            p.wait("vector", t_ln)
            TS(p, dAf[0:64, :], dtv[0:64, :], nA[:, 0:1], None, ALU.mult)
            SCAN(p, cum[0:64, :], g.mc128[0:64, 0:W], dAf[0:64, :], 0.0,
                 ALU.mult, ALU.add)
            SCAN(p, rvc[0:64, ::-1], g.mc128r[0:64, W - 1::-1], dAf[0:64, ::-1],
                 0.0, ALU.mult, ALU.add)
            TT(p, rvc[0:64, :], rvc[0:64, :], dAf[0:64, :], ALU.subtract)
            t_d2p = TT(p, d2e[0:64, :], rvc[0:64, :], lndt[0:64, :], ALU.add)
            p.wait("scalar", t_d2p)
            t_d2 = ACT(p, d2e[0:64, :], d2e[0:64, :], AF.Exp)
            t_ee = ACT(p, eend[0:64, :], cum[0:64, 127::128], AF.Exp)
            p.wait("vector", t_ee)
            p.barrier()
            if DEBUG_TAP == "s4" and tt == 0:
                db = d["t_dbg"].bitcast(F32)
                for i, src in enumerate((dtv, lndt, dAf, cum, rvc, d2e)):
                    p.st(db[i * 64:(i + 1) * 64, 0:W], src[0:64, :])
                p.barrier()

            for ch in range(NCH):
                cc0 = ch * 128
                gc0 = ch * 128 + c0
                zsel = 0
                if ch > 0 or tt > 0:
                    p.wait("tensor", ch_ac)
                    p.wait("tensor", ch_dv)
                    p.wait("vector", ch_ac)
                if zld_g.get(zsel) is not None:
                    p.wait("sync", zld_g[zsel])
                t_z = p.ld(zt[zsel][:], d["zs"][gc0:gc0 + 128, :])
                tks = []
                for (src, dst) in ((dAf, tmA), (lndt, tmL), (cum, tmE), (d2e, tmD)):
                    t_t = TRN(p, g.ps[7][:, 0:64], src[0:64, cc0:cc0 + 128], g.idn[0:64, 0:64])
                    p.wait("vector", t_t)
                    if dst is tmE:
                        p.wait("scalar", t_t)
                        t_c = ACT(p, dst[:], g.ps[7][:, 0:64], AF.Exp)
                        p.wait("vector", t_c)
                        p.wait("tensor", t_c)
                    else:
                        t_c = CP(p, dst[:], g.ps[7][:, 0:64])
                        p.wait("tensor", t_c)
                    tks.append(t_c)
                t_t = TRN(p, g.ps[7][0:1, 64:128], eend[0:64, ch:ch + 1],
                          g.idn[0:64, 0:64])
                p.wait("vector", t_t)
                t_er = CP(p, erow[:], g.ps[7][0:1, 64:128])
                p.wait("tensor", t_er)
                MM(p, g.ps[7][:, 128:160], mtop[:, :], erow[0:1, 0::2], True, False,
                   inc=False)
                t_hd = MM(p, g.ps[7][:, 128:160], mbot[:, :], erow[0:1, 1::2],
                          False, True)
                p.wait("vector", t_hd)
                t_hdc = CP(p, hdecp[:], g.ps[7][:, 128:160])
                p.wait("tensor", t_hdc)
                for par in range(2):
                    bk = g.ps[5 + par]
                    for q in range(8):
                        gi2 = 2 * q + par
                        pc = gi2 // 2
                        base = 64 * (gi2 % 2)
                        t_t = TRN(p, bk.bitcast(BF16)[:, q * 64:(q + 1) * 64],
                                  x_sb[base:base + 64, pc * W + cc0:pc * W + cc0 + 128],
                                  g.idnb[base:base + 64, base:base + 64])
                    p.wait("vector", t_t)
                    t_c = CP(p, x_tm[:].rearrange("p (gg n) -> p gg n", gg=16)[:, par::2, :],
                             bk.bitcast(BF16)[:, 0:512].rearrange("p (q n) -> p q n", q=8))
                    p.wait("tensor", t_c)
                for par in range(2):
                    bk = g.ps[5 + par]
                    for q in range(8):
                        gi2 = 2 * q + par
                        pc = gi2 // 2
                        base = 64 * (gi2 % 2)
                        t_t = TRN(p, bk.bitcast(BF16)[:, q * 64:(q + 1) * 64],
                                  b_sb[base:base + 64, pc * W + cc0:pc * W + cc0 + 128],
                                  g.idnb[base:base + 64, base:base + 64])
                    p.wait("vector", t_t)
                    t_c = CP(p, b_tm[:].rearrange("p (gg n) -> p gg n", gg=16)[:, par::2, :],
                             bk.bitcast(BF16)[:, 0:512].rearrange("p (q n) -> p q n", q=8))
                    p.wait("tensor", t_c)
                for par in range(2):
                    src_base = 64 * par
                    in_ap = b_sb[src_base:src_base + 64, :].rearrange(
                        "p (a w) -> p a w", a=8)[:, :, cc0:cc0 + 128]
                    out0 = bdup[0:64, :].rearrange("p (a j) -> p a j", a=16, j=128)[
                        :, par::2, :]
                    out1 = bdup[64:128, :].rearrange("p (a j) -> p a j", a=16, j=128)[
                        :, par::2, :]
                    CP(p, out0, in_ap)
                    t_bd = CP(p, out1, in_ap)
                p.wait("tensor", t_bd)

                grp_dv = None
                grp_ac = None
                for g4 in range(16):
                    if grp_dv is not None:
                        p.wait("tensor", grp_dv)
                        p.wait("tensor", grp_ac)
                    t_diff = None
                    t_lda = p.dv(lambda e, o=lda4[:].rearrange("p (h i) -> p h i", h=4),
                                 a=g.ltri[:].unsqueeze(1).broadcast_to([128, 4, 128]),
                                 b=tmA[:, 4 * g4:4 * g4 + 4].unsqueeze(2).broadcast_to([128, 4, 128]):
                                 e.tensor_tensor(o, a, b, ALU.mult))
                    p.wait("vector", t_lda)
                    p.wait("tensor", t_lda)
                    for j in range(4):
                        MM(p, g.ps[0][:, j * 128:(j + 1) * 128], g.ustrict[:],
                           lda4[:, j * 128:(j + 1) * 128], True, False, inc=False)
                        t_diff = MM(p, g.ps[0][:, j * 128:(j + 1) * 128], g.idn[:],
                                    g.negbig[:], False, True)
                    t_mx = None
                    p.wait("scalar", t_diff)
                    for j in range(4):
                        t_mx = ACT(p, mexp4[:, j * 128:(j + 1) * 128],
                                   g.ps[0][:, j * 128:(j + 1) * 128], AF.Exp,
                                   bias=tmL[:, 4 * g4 + j:4 * g4 + j + 1])
                    t_gt = {}
                    for par in range(2):
                        bk = g.ps[1 + par]
                        for l in range(2):
                            j = 2 * l + par
                            h = 4 * g4 + j
                            beta = 64 * (j % 2)
                            pc = h // 2
                            t_gt[par] = MM(p, bk[:, l * 128:(l + 1) * 128],
                                           bdup[beta:beta + 64, g4 * 128:(g4 + 1) * 128],
                                           c_sb[beta:beta + 64,
                                                pc * W + cc0:pc * W + cc0 + 128],
                                           True, True)
                    p.wait("vector", t_mx)
                    p.wait("vector", t_gt[0])
                    t_gma = TT(p, gma[:].rearrange("p (l i) -> p l i", l=2),
                               g.ps[1][:, 0:256].rearrange("p (l i) -> p l i", l=2),
                               mexp4[:].rearrange("p (h i) -> p h i", h=4)[:, 0::2, :],
                               ALU.mult)
                    p.wait("vector", t_gt[1])
                    t_gmb = TT(p, gmb[:].rearrange("p (l i) -> p l i", l=2),
                               g.ps[2][:, 0:256].rearrange("p (l i) -> p l i", l=2),
                               mexp4[:].rearrange("p (h i) -> p h i", h=4)[:, 1::2, :],
                               ALU.mult)
                    t_bd4 = p.dv(lambda e, o=bd4[:].rearrange("p (h n) -> p h n", h=4),
                                 a=b_tm[:, g4 * 64:(g4 + 1) * 64].unsqueeze(1).broadcast_to([128, 4, 64]),
                                 b=tmD[:, 4 * g4:4 * g4 + 4].unsqueeze(2).broadcast_to([128, 4, 64]):
                                 e.tensor_tensor(o, a, b, ALU.mult))
                    p.wait("vector", t_bd4)
                    p.wait("tensor", t_bd4)
                    p.wait("tensor", t_gma)
                    p.wait("tensor", t_gmb)
                    t_y1 = None
                    for j in range(4):
                        gm = (gma if j % 2 == 0 else gmb)
                        l = j // 2
                        t_y1 = MM(p, g.ps[3][:, j * 64:(j + 1) * 64],
                                  gm[:, l * 128:(l + 1) * 128],
                                  x_tm[:, g4 * 64:(g4 + 1) * 64], True, True)
                    t_s = None
                    for j in range(4):
                        beta = 64 * (j % 2)
                        l = j // 2
                        t_s = MM(p, g.ps[4][beta:beta + 64, l * 64:(l + 1) * 64],
                                 bd4[:, j * 64:(j + 1) * 64],
                                 x_tm[:, g4 * 64:(g4 + 1) * 64], True, True,
                                 tp=(0, beta))
                    t_hb = CP(p, hb16[:], Ht[:, g4 * 128:(g4 + 1) * 128])
                    p.wait("tensor", t_hb)
                    t_y2 = {}
                    for par in range(2):
                        bk = g.ps[5 + par]
                        for l in range(2):
                            j = 2 * l + par
                            h = 4 * g4 + j
                            beta = 64 * (j % 2)
                            pc = h // 2
                            t_y2[par] = MM(p, bk[:, l * 64:(l + 1) * 64],
                                           c_sb[beta:beta + 64,
                                                pc * W + cc0:pc * W + cc0 + 128],
                                           hb16[beta:beta + 64, l * 64:(l + 1) * 64],
                                           True, True)
                    p.wait("vector", t_y1)
                    t_xd = TT(p, w1t[:].rearrange("p (h n) -> p h n", h=4),
                              x_tm[:, g4 * 64:(g4 + 1) * 64].unsqueeze(1).broadcast_to([128, 4, 64]),
                              dcol[:, 4 * g4:4 * g4 + 4].unsqueeze(2).broadcast_to([128, 4, 64]),
                              ALU.mult)
                    t_2 = TT(p, w2t[:], g.ps[3][:, 0:256], w1t[:], ALU.add)
                    p.wait("vector", t_y2[0])
                    t_1a = TT(p, w3a[:].rearrange("p (l n) -> p l n", l=2),
                              g.ps[5][:, 0:128].rearrange("p (l n) -> p l n", l=2),
                              tmE[:, 4 * g4:4 * g4 + 4].unsqueeze(2)
                              .broadcast_to([128, 4, 64])[:, 0::2, :], ALU.mult)
                    p.wait("vector", t_y2[1])
                    t_1b = TT(p, w3b[:].rearrange("p (l n) -> p l n", l=2),
                              g.ps[6][:, 0:128].rearrange("p (l n) -> p l n", l=2),
                              tmE[:, 4 * g4:4 * g4 + 4].unsqueeze(2)
                              .broadcast_to([128, 4, 64])[:, 1::2, :], ALU.mult)
                    TT(p, Yt[:, 4 * g4 * 64:(4 * g4 + 4) * 64]
                       .rearrange("p (h n) -> p h n", h=4)[:, 0::2, :],
                       w3a[:].rearrange("p (l n) -> p l n", l=2),
                       w2t[:].rearrange("p (h n) -> p h n", h=4)[:, 0::2, :], ALU.add)
                    t_yo = TT(p, Yt[:, 4 * g4 * 64:(4 * g4 + 4) * 64]
                              .rearrange("p (h n) -> p h n", h=4)[:, 1::2, :],
                              w3b[:].rearrange("p (l n) -> p l n", l=2),
                              w2t[:].rearrange("p (h n) -> p h n", h=4)[:, 1::2, :], ALU.add)
                    t_hm = TT(p, hmt[:].rearrange("p (m n) -> p m n", m=2),
                              Ht[:, g4 * 128:(g4 + 1) * 128].rearrange("p (m n) -> p m n", m=2),
                              hdecp[:, 2 * g4:2 * g4 + 2].unsqueeze(2)
                              .broadcast_to([128, 2, 64]),
                              ALU.mult)
                    p.wait("vector", t_s)
                    t_hu = TT(p, Ht[:, g4 * 128:(g4 + 1) * 128], hmt[:],
                              g.ps[4][:, 0:128], ALU.add)
                    grp_dv = t_hu
                    grp_ac = t_mx
                    if DEBUG_TAP == "s4" and tt == 0 and ch == 0 and g4 == 0:
                        p.barrier()
                        db = d["t_dbg"]
                        p.st(db.bitcast(F32)[640:768, 0:512], lda4[:])
                        p.st(db[768:896, 0:512], mexp4[:])
                        p.st(db[896:1024, 0:256], gma[:])
                        p.st(db[896:1024, 256:512], gmb[:])
                        p.st(db[1024:1152, 0:1024], x_tm[:])
                        p.st(db[1152:1280, 0:1024], b_tm[:])
                        p.st(db[1280:1408, 0:2048], bdup[:])
                        p.st(db[1408:1536, 0:256], bd4[:])
                        p.barrier()
                if DEBUG_TAP == "s4" and tt == 0 and ch == 0:
                    p.barrier()
                    p.st(d["t_dbg"][1536:1664, 0:4096], Yt[:])
                    p.barrier()
                p.wait("vector", t_z)
                t_yz = TT(p, zt[zsel][:], zt[zsel][:], Yt[:], ALU.mult)
                zld_g[zsel] = t_yz
                p.wait("scalar", t_yz)
                t_sq4 = ACT(p, Yt[:], zt[zsel][:], AF.Square)
                p.wait("vector", t_sq4)
                t_ss = RED(p, ssq[:], Yt[:].rearrange("p (h n) -> p h n", h=64), ALU.add)
                p.wait("scalar", t_ss)
                ACT(p, rstd[:], ssq[:], AF.Ln, bias=1e-5, scale=1.0 / 64)
                t_rs = ACT(p, rstd[:], rstd[:], AF.Exp, scale=-0.5)
                p.wait("vector", t_rs)
                if DEBUG_TAP == "s4" and tt == 0 and ch == 0:
                    p.barrier()
                    p.st(d["t_dbg"].bitcast(F32)[1664:1792, 0:64], ssq[:])
                    p.st(d["t_dbg"].bitcast(F32)[1664:1792, 64:128], rstd[:])
                    p.barrier()
                t_nm = TT(p, Yt[:].rearrange("p (h n) -> p h n", h=64),
                          zt[zsel][:].rearrange("p (h n) -> p h n", h=64),
                          rstd[:].unsqueeze(2).broadcast_to([128, 64, 64]),
                          ALU.mult)
                zld_g[zsel] = t_nm
                p.wait("tensor", t_nm)
                for qb in range(8):
                    t_t = None
                    for qq in range(4):
                        ocn = qb * 4 + qq
                        t_t = TRN(p, g.ps[7].bitcast(BF16)[:, qq * 128:(qq + 1) * 128],
                                  Yt[:, ocn * 128:(ocn + 1) * 128], g.idnb[:])
                    p.wait("scalar", t_t)
                    t_ev = p.ac(lambda e, o=nrm[:].rearrange("p (c w) -> p c w", w=W)[
                        :, qb * 4:qb * 4 + 4, cc0:cc0 + 128],
                        i=g.ps[7].bitcast(BF16)[:, 0:512].rearrange(
                            "p (c j) -> p c j", c=4):
                        e.activation(o, i, AF.Copy))
                    p.wait("tensor", t_ev)
                grp_dv = t_nm
                grp_ac = t_ev
                ch_dv = t_nm
                ch_ac = t_ev
                if DEBUG_TAP == "s4" and tt == 0 and ch == 0:
                    p.barrier()
                    p.st(d["t_dbg"][384:512, 0:4096], Yt[:])
                    p.st(d["t_dbg"][512:640, 0:4096], zt[zsel][:])
                    p.barrier()
            p.barrier()

            dn = {}
            wo_g = None
            for ocn in range(NKC):
                bsel = ocn % 2
                if wo_g is not None:
                    p.wait("sync", wo_g)
                t_w = p.ld(wob[bsel][:].rearrange("p (c o) -> p c o", o=128),
                           d["wout_bf"][:, ocn * 128:(ocn + 1) * 128]
                           .rearrange("(c p) o -> p c o", p=128))
                bk = g.ps[ocn % 4]
                if ocn >= 4:
                    p.wait("tensor", dn[ocn - 4])
                p.wait("tensor", t_w)
                t_mm = mm_chain(p, bk[:, :],
                                [(wob[bsel][:, kc * 128:(kc + 1) * 128],
                                  nrm[:, kc * W:(kc + 1) * W]) for kc in range(32)])
                wo_g = t_mm
                p.wait("scalar", t_mm)
                t_e = ACT(p, delta[ocn], bk[:, :], AF.Copy)
                dn[ocn] = t_e
            p.barrier()

            st_tok = None
            for k in range(NKC):
                st_tok = ln_stats_chunk(p, g, delta[k], sqb, W, k == 0, k == NKC - 1,
                                        war=st_tok)
            bc = ln_rows_and_bcast(p, g, W, D, st_tok)
            for k in range(NKC):
                xr = xres[k % 2]
                t_x = p.ld(xr[:], d["x2"][k * 128:(k + 1) * 128, c0:c0 + W])
                p.wait("vector", t_x)
                t_n = ln_resid_chunk(p, g, xnew, delta[k], xr, scr, W, bc)
                p.wait("sync", t_n)
                t_st = p.st(d["x3"][k * 128:(k + 1) * 128, c0:c0 + W], xnew[:])
                p.wait("vector", t_st)
            p.barrier()



class LazyDram(dict):

    def __init__(self, nc, scratch_ext):
        super().__init__()
        self.nc = nc
        self.scratch_ext = scratch_ext

    def __missing__(self, name):
        if name == "x0":
            ap = self.nc.dram_tensor("x0", [D, T], F32, kind="ExternalInput").ap()
        elif name == "out":
            ap = self.nc.dram_tensor("out", [D, T], F32, kind="ExternalOutput").ap()
        elif name == "t_dbg":
            ap = self.nc.dram_tensor("t_dbg", [D, T], BF16, kind="ExternalOutput").ap()
        elif name in WSPECS:
            shape, dt = WSPECS[name]
            ap = self.nc.dram_tensor(name, shape, dt, kind="ExternalInput").ap()
        else:
            shape, dt = SCRATCH[name]
            kind = {"in": "ExternalInput", "out": "ExternalOutput"}.get(
                self.scratch_ext.get(name), "Internal")
            ap = self.nc.dram_tensor(name, shape, dt, kind=kind).ap()
        self[name] = ap
        return ap


def build_nc(stages, scratch_ext):
    nc = bass.Bass("TRN2", target_bir_lowering=False)
    d = LazyDram(nc, scratch_ext)
    with ExitStack() as es:
        p = Prog(nc, es)
        g = _setup_globals(p, d, es)
        for s in stages:
            s(p, g)
            p.barrier()
        p.check_deadlock()
        p.finalize()
    return nc


def _declared_inputs(nc):
    names = []
    for alloc in nc.m.functions[0].allocations:
        if isinstance(alloc, mybir.MemoryLocationSet) and alloc.kind == "ExternalInput":
            names.append(alloc.memorylocations[0].name)
    return names


def run_stages(stages, scratch_ext, per_core_maps):
    nc = build_nc(stages, scratch_ext)
    need = set(_declared_inputs(nc))
    in_maps = [{k: v for k, v in m.items() if k in need} for m in per_core_maps]
    res = run_bass_kernel_spmd(nc, in_maps, core_ids=list(range(len(in_maps))))
    return res.results


FUSED = True


def kernel(**inputs):
    w = _prep_weights(inputs)
    x = inputs["L0_attn_x"]
    ncores = 8
    per_core = []
    for b in range(ncores):
        m = dict(w)
        m["x0"] = np.ascontiguousarray(x[b].T.astype(np.float32))
        per_core.append(m)
    if FUSED:
        stages = [emit_s1, emit_s2, emit_s3, emit_s4, emit_s5a, emit_s5b]
        results = run_stages(stages, {}, per_core)
        return np.stack([r["out"].T for r in results], axis=0)
    launches = [
        ([emit_s1], {"x1": "out"}),
        ([emit_s2], {"x1": "in", "x2": "out"}),
        ([emit_s3], {"x2": "in", "zs": "out", "xbc": "out", "dtr": "out"}),
        ([emit_s4], {"x2": "in", "zs": "in", "xbc": "in", "dtr": "in",
                     "x3": "out"}),
        ([emit_s5a], {"x3": "in", "hxp": "out", "wfm": "out"}),
        ([emit_s5b], {"x3": "in", "hxp": "in", "wfm": "in"}),
    ]
    import gc
    drop_after = {1: ["x1"], 3: ["zs", "xbc", "dtr", "x2"], 5: ["x3", "hxp", "wfm"]}
    for li, (stages, ext) in enumerate(launches):
        results = run_stages(stages, ext, per_core)
        for b in range(ncores):
            for name, kind in ext.items():
                if kind == "out":
                    per_core[b][name] = results[b][name]
        for name in drop_after.get(li, []):
            for b in range(ncores):
                per_core[b].pop(name, None)
        if li < 5:
            results = None
        gc.collect()
    return np.stack([r["out"].T for r in results], axis=0)


# version 04 (final)
# speedup vs baseline: 1.0512x; 1.0512x over previous
import numpy as np
from contextlib import ExitStack

import concourse.bass as bass
import concourse.mybir as mybir
from concourse.bass_utils import run_bass_kernel_spmd

F32 = mybir.dt.float32
BF16 = mybir.dt.bfloat16
AF = mybir.ActivationFunctionType
ALU = mybir.AluOpType
AX = mybir.AxisListType

ACT_CONSTS = [0.0, 1.0, -1.0, 0.5, -0.5, 2.0, 1e-5, 1.1920928955078125e-07,
              1.0 / 64, 1.0 / 128, 1.0 / 2048, 1e6, 0.6, 0.9995, -60.0,
              1.0 / 4096, 64.0, 128.0]

DEBUG_TAP = None
S5A_PHASES = ("router", "experts", "shared")

T = 4096
D = 2048
NKC = D // 128
LN_EPS = 1e-5

ENGINES = ("sync", "gpsimd", "tensor", "vector", "scalar")
SEMS = ("LD", "CA", "PE", "DV", "AC", "ST")


class Prog:

    def __init__(self, nc, es):
        self.nc = nc
        self.es = es
        self.ops = {e: [] for e in ENGINES}
        self.sems = {}
        self.cnt = {}
        for s in SEMS:
            self.sems[s] = es.enter_context(nc.semaphore(s))
            self.cnt[s] = 0
        self._pw = {e: [] for e in ENGINES}
        self.meta = {e: [] for e in ENGINES}

    def wait(self, engine, tok):
        if tok is not None:
            self._pw[engine].append(tok)

    def _record(self, engine, fn, inc_sem=None, inc_n=1):
        waits = self._pw[engine]
        self._pw[engine] = []
        sems = self.sems

        def run(eng, waits=waits, fn=fn, inc_sem=inc_sem, inc_n=inc_n):
            seen = {}
            for (s, v) in waits:
                if seen.get(s, -1) < v:
                    seen[s] = v
            for s, v in seen.items():
                eng.wait_ge(sems[s], v)
            inst = fn(eng)
            if inc_sem is not None:
                inst.then_inc(sems[inc_sem], inc_n)
        self.ops[engine].append(run)
        wd = {}
        for (sn, v) in waits:
            wd[sn] = max(wd.get(sn, 0), v)
        self.meta[engine].append((wd, inc_sem, inc_n))
        if inc_sem is not None:
            self.cnt[inc_sem] += inc_n
            return (inc_sem, self.cnt[inc_sem])
        return None

    def ld(self, out, in_):
        return self._record("sync", lambda e, o=out, i=in_: e.dma_start(out=o, in_=i), "LD", 16)

    def st(self, out, in_):
        return self._record("sync", lambda e, o=out, i=in_: e.dma_start(out=o, in_=i), "ST", 16)

    def ca(self, out, in_):
        return self._record("gpsimd", lambda e, o=out, i=in_: e.dma_start(out=o, in_=i), "CA", 16)

    def pe(self, fn, inc=True):
        return self._record("tensor", fn, "PE" if inc else None)

    def dv(self, fn):
        tok = self._record("vector", fn, "DV")
        self.wait("vector", tok)
        return tok

    def ac(self, fn):
        return self._record("scalar", fn, "AC")

    def tok(self, sem):
        return (sem, self.cnt[sem])

    def barrier(self):
        toks = [(s, self.cnt[s]) for s in SEMS if self.cnt[s] > 0]
        for e in ENGINES:
            for t in toks:
                self.wait(e, t)
            self._record(e, lambda eng: eng.nop())

    def check_deadlock(self):
        cnt = {sname: 0 for sname in SEMS}
        idx = {e: 0 for e in ENGINES}
        total = sum(len(v) for v in self.meta.values())
        done = 0
        while done < total:
            progress = False
            for e in ENGINES:
                while idx[e] < len(self.meta[e]):
                    waits, inc_sem, inc_n = self.meta[e][idx[e]]
                    if any(cnt[sn] < v for sn, v in waits.items()):
                        break
                    if inc_sem is not None:
                        cnt[inc_sem] += inc_n
                    idx[e] += 1
                    done += 1
                    progress = True
            if not progress:
                for e in ENGINES:
                    if idx[e] < len(self.meta[e]):
                        waits, _, _ = self.meta[e][idx[e]]
                        blocked = {sn: (v, cnt[sn]) for sn, v in waits.items()
                                   if cnt[sn] < v}
                        print(f"DEADLOCK {e} op {idx[e]}/{len(self.meta[e])} "
                              f"blocked on {blocked}")
                raise RuntimeError("static deadlock detected")
        return True

    def finalize(self):
        block = self.es.enter_context(self.nc.Block())
        for e in ENGINES:
            ops = self.ops[e]
            getattr(block, e)(lambda eng, ops=ops: [f(eng) for f in ops])



def MM(p, out, l, r, start, stop, tp=None, inc=True):
    return p.pe(lambda e, o=out, l=l, r=r, a=start, b=stop, tp=tp:
                e.matmul(o, l, r, start=a, stop=b, tile_position=tp), inc=inc)


def mm_chain(p, out, pairs, tp=None, inc=True):
    n = len(pairs)
    tok = None
    for i, (l, r) in enumerate(pairs):
        tok = MM(p, out, l, r, i == 0, i == n - 1, tp=tp,
                 inc=(inc and i == n - 1))
    return tok


def TT(p, out, a, b, op):
    return p.dv(lambda e, o=out, x=a, y=b, op=op: e.tensor_tensor(o, x, y, op))


def TS(p, out, a, s1, s2, op0, op1=None):
    if s2 is None:
        return p.dv(lambda e, o=out, x=a, s=s1, op=op0:
                    e.tensor_single_scalar(o, x, s, op))
    return p.dv(lambda e, o=out, x=a, s1=s1, s2=s2, op0=op0, op1=op1:
                e.tensor_scalar(o, x, s1, s2, op0, op1))


def STT(p, out, in0, sc, in1, op0, op1):
    return p.dv(lambda e, o=out, x=in0, s=sc, y=in1, a=op0, b=op1:
                e.scalar_tensor_tensor(o, x, s, y, a, b))


def CP(p, out, in_):
    return p.dv(lambda e, o=out, i=in_: e.tensor_copy(o, i))


def ACT(p, out, in_, func, bias=0.0, scale=1.0):
    return p.ac(lambda e, o=out, i=in_, f=func, b=bias, s=scale:
                e.activation(o, i, f, bias=b, scale=s))


def SCAN(p, out, d0, d1, init, op0, op1):
    tok = p.dv(lambda e, o=out, a=d0, b=d1, i=init, x=op0, y=op1:
               e.tensor_tensor_scan(o, a, b, i, x, y))
    p.wait("vector", tok)
    return tok


def RED(p, out, in_, op):
    tok = p.dv(lambda e, o=out, i=in_, op=op: e.tensor_reduce(o, i, AX.X, op))
    p.wait("vector", tok)
    return tok


def TRN(p, out, in_, ident):
    return p.pe(lambda e, o=out, i=in_, d=ident: e.transpose(o, i, d))



def _host_consts():
    c = {}
    c["c_idn"] = np.eye(128, dtype=np.float32)
    tvec = np.arange(1024)
    c["c_m0_64"] = np.broadcast_to((tvec % 64 != 0).astype(np.float32), (128, 1024)).copy()
    c["c_m1_64"] = np.broadcast_to((tvec % 64 == 0).astype(np.float32), (128, 1024)).copy()
    c["c_mc128"] = np.broadcast_to((tvec % 128 != 0).astype(np.float32), (128, 1024)).copy()
    c["c_mc128r"] = np.broadcast_to((tvec % 128 != 127).astype(np.float32), (128, 1024)).copy()
    i = np.arange(128)
    c["c_ltri"] = (i[:, None] <= i[None, :]).astype(np.float32)
    c["c_ustrict"] = (i[:, None] > i[None, :]).astype(np.float32)
    c["c_negbig"] = np.where(i[None, :] < i[:, None], -60.0, 0.0).astype(np.float32)
    c["c_ones"] = np.ones((128, 128), dtype=np.float32)
    c["c_scal"] = np.broadcast_to(np.array(ACT_CONSTS, dtype=np.float32), (128, len(ACT_CONSTS))).copy()
    mt = np.zeros((1, 128), dtype=np.float32); mt[0, :64] = 1.0
    mb = np.zeros((1, 128), dtype=np.float32); mb[0, 64:] = 1.0
    c["c_mtop"] = mt
    c["c_mbot"] = mb
    sel = np.zeros((8, 8 * 128), dtype=np.float32)
    for e in range(8):
        sel[e, e * 128:(e + 1) * 128] = 1.0
    c["c_sel"] = sel
    return c


def _prep_weights(inp):
    w = {}
    f32 = np.float32

    def t2(a):
        return np.ascontiguousarray(a.T.astype(f32, copy=False))
    w["w0_in"] = t2(inp["L0_attn_w_in"])
    w["w0_out"] = t2(inp["L0_attn_w_out"])
    w["w0_g"] = t2(inp["L0_ffn_gate_w"])
    w["w0_u"] = t2(inp["L0_ffn_up_w"])
    w["w0_d"] = t2(inp["L0_ffn_down_w"])
    w["w0_gs"] = np.ascontiguousarray(inp["L0_ffn_gate_s"].astype(f32).reshape(44, 128).T)
    w["w0_us"] = np.ascontiguousarray(inp["L0_ffn_up_s"].astype(f32).reshape(44, 128).T)
    w["w0_ds"] = np.ascontiguousarray(inp["L0_ffn_down_s"].astype(f32).reshape(16, 128).T)
    w["w1_in"] = np.ascontiguousarray(inp["L1_attn_W_in"].astype(f32))
    w["w1_cw"] = np.ascontiguousarray(
        inp["L1_attn_conv_w"].astype(f32).reshape(48, 128, 4).transpose(1, 0, 2)
        .reshape(128, 192))
    w["w1_cb"] = np.ascontiguousarray(inp["L1_attn_conv_b"].astype(f32).reshape(48, 128).T)
    w["w1_dtb"] = inp["L1_attn_dt_bias"].astype(f32).reshape(64, 1)
    w["w1_alog"] = inp["L1_attn_A_log"].astype(f32).reshape(64, 1)
    w["w1_d64"] = np.ascontiguousarray(inp["L1_attn_D"].astype(f32).reshape(1, 64))
    w["w1_nw"] = np.ascontiguousarray(inp["L1_attn_norm_weight"].astype(f32).reshape(32, 128).T)
    w["w1_out"] = np.ascontiguousarray(inp["L1_attn_W_out"].astype(f32))
    w["w2_wg"] = np.ascontiguousarray(inp["L1_ffn_Wg"].astype(f32))
    w["w2_bg"] = inp["L1_ffn_bg"].astype(f32).reshape(1, 8)
    w["w2_wgate"] = np.ascontiguousarray(inp["L1_ffn_Wgate"].astype(f32).reshape(12288, 1024))
    w["w2_bgate"] = np.ascontiguousarray(
        inp["L1_ffn_bgate"].astype(f32).reshape(48, 128).T)
    w["w2_wup"] = np.ascontiguousarray(inp["L1_ffn_Wup"].astype(f32).reshape(12288, 1024))
    w["w2_bup"] = np.ascontiguousarray(inp["L1_ffn_bup"].astype(f32).reshape(48, 128).T)
    w["w2_wdown"] = np.ascontiguousarray(inp["L1_ffn_Wdown"].astype(f32).reshape(6144, 2048))
    w["w2_bdown"] = np.ascontiguousarray(inp["L1_ffn_bdown"].astype(f32))
    w["w2_wsu"] = np.ascontiguousarray(inp["L1_ffn_Ws_up"].astype(f32))
    w["w2_bsu"] = np.ascontiguousarray(inp["L1_ffn_bs_up"].astype(f32).reshape(16, 128).T)
    w["w2_wsd"] = np.ascontiguousarray(inp["L1_ffn_Ws_down"].astype(f32))
    w["w2_bsd"] = np.ascontiguousarray(inp["L1_ffn_bs_down"].astype(f32).reshape(16, 128).T)
    w.update(_host_consts())
    return w


WSPECS = {
    "w0_in": ([D, 10240], F32), "w0_out": ([D, D], F32),
    "w0_g": ([D, 5632], F32), "w0_u": ([D, 5632], F32), "w0_d": ([5632, D], F32),
    "w0_gs": ([128, 44], F32), "w0_us": ([128, 44], F32), "w0_ds": ([128, 16], F32),
    "w1_in": ([D, 10304], F32), "w1_cw": ([128, 192], F32), "w1_cb": ([128, 48], F32),
    "w1_dtb": ([64, 1], F32), "w1_alog": ([64, 1], F32), "w1_d64": ([1, 64], F32),
    "w1_nw": ([128, 32], F32), "w1_out": ([4096, D], F32),
    "w2_wg": ([D, 8], F32), "w2_bg": ([1, 8], F32),
    "w2_wgate": ([12288, 1024], F32), "w2_bgate": ([128, 48], F32),
    "w2_wup": ([12288, 1024], F32), "w2_bup": ([128, 48], F32),
    "w2_wdown": ([6144, D], F32), "w2_bdown": ([6, D], F32),
    "w2_wsu": ([D, D], F32), "w2_bsu": ([128, 16], F32),
    "w2_wsd": ([D, D], F32), "w2_bsd": ([128, 16], F32),
    "c_idn": ([128, 128], F32),
    "c_m0_64": ([128, 1024], F32), "c_m1_64": ([128, 1024], F32),
    "c_mc128": ([128, 1024], F32), "c_mc128r": ([128, 1024], F32),
    "c_ltri": ([128, 128], F32), "c_ustrict": ([128, 128], F32),
    "c_negbig": ([128, 128], F32), "c_ones": ([128, 128], F32),
    "c_sel": ([8, 1024], F32), "c_mtop": ([1, 128], F32), "c_mbot": ([1, 128], F32),
    "c_scal": ([128, len(ACT_CONSTS)], F32),
}

SCRATCH = {
    "x1": ([D, T], F32), "x2": ([D, T], F32), "x3": ([D, T], F32),
    "zs": ([T, 4096], BF16),
    "xbc": ([6144, T + 3], BF16),
    "dtr": ([64, T], F32),
    "hxp": ([8192, T], BF16),
    "wfm": ([8, T], F32),
    "wout_bf": ([4096, D], BF16),
}


class G:
    pass


def _setup_globals(p, d, es):
    nc = p.nc
    g = G()
    g.d = d
    g.ps = [es.enter_context(nc.psum_tensor(f"psb{i}", [128, 512], F32))
            for i in range(8)]

    def sb(name, shape, dt=F32):
        return es.enter_context(nc.sbuf_tensor(name, shape, dt))
    g.sb = sb
    g.idn = sb("g_idn", [128, 128])
    g.idnb = sb("g_idnb", [128, 128], BF16)
    g.onesb = sb("g_onesb", [128, 128], BF16)
    g.ones = sb("g_ones", [128, 128])
    g.ltri = sb("g_ltri", [128, 128])
    g.ustrict = sb("g_ustrict", [128, 128])
    g.negbig = sb("g_negbig", [128, 128])
    g.ustrictb = sb("g_ustrictb", [128, 128], BF16)
    g.negbigb = sb("g_negbigb", [128, 128], BF16)
    g.m0_64 = sb("g_m0", [128, 1024])
    g.m1_64 = sb("g_m1", [128, 1024])
    g.mc128 = sb("g_mc", [128, 1024])
    g.mc128r = sb("g_mcr", [128, 1024])
    g.rows = sb("g_rows", [1, 4096])
    g.scal = sb("g_scal", [128, len(ACT_CONSTS)])
    g.scalb = sb("g_scalb", [128, len(ACT_CONSTS)], BF16)
    p.ld(g.idn[:], d["c_idn"][:])
    p.ca(g.idnb[:], d["c_idn"][:])
    p.ca(g.onesb[:], d["c_ones"][:])
    p.ld(g.ones[:], d["c_ones"][:])
    p.ld(g.ltri[:], d["c_ltri"][:])
    p.ld(g.ustrict[:], d["c_ustrict"][:])
    p.ld(g.negbig[:], d["c_negbig"][:])
    p.ca(g.ustrictb[:], d["c_ustrict"][:])
    p.ca(g.negbigb[:], d["c_negbig"][:])
    p.ld(g.m0_64[:], d["c_m0_64"][:])
    p.ld(g.m1_64[:], d["c_m1_64"][:])
    p.ld(g.mc128[:], d["c_mc128"][:])
    p.ld(g.mc128r[:], d["c_mc128r"][:])
    p.ld(g.scal[:], d["c_scal"][:])
    p.ca(g.scalb[:], d["c_scal"][:])
    for j, v in enumerate(ACT_CONSTS):
        nc.const_aps.register(F32, v, g.scal[:, j:j + 1])
        nc.const_aps.register(BF16, v, g.scalb[:, j:j + 1])
    p.barrier()
    return g



def ln_stats_chunk(p, g, src_b16, sqb, W, first, last, src_tok=None, war=None):
    if src_tok is not None:
        p.wait("scalar", src_tok)
    if war is not None:
        p.wait("scalar", war)
    t_sq = p.ac(lambda e, o=sqb[:, 0:W], i=src_b16[:, 0:W]:
                e.activation(o, i, AF.Square))
    tok = None
    for s in range(W // 512):
        if src_tok is not None:
            p.wait("tensor", src_tok)
        MM(p, g.ps[s][0:1, :], g.onesb[:, 0:1], src_b16[:, s * 512:(s + 1) * 512],
           first, last, inc=False)
        p.wait("tensor", t_sq)
        tok = MM(p, g.ps[2 + s][0:1, :], g.onesb[:, 0:1], sqb[:, s * 512:(s + 1) * 512],
                 first, last, inc=True)
    return tok


def ln_rows_and_bcast(p, g, W, n_feat, stat_tok):
    inv = 1.0 / n_feat
    R = g.rows
    tok = None
    for s in range(W // 512):
        mu = R[0:1, 0:512]
        ms = R[0:1, 512:1024]
        mu2 = R[0:1, 1024:1536]
        var = R[0:1, 1536:2048]
        rstd = R[0:1, 2048 + s * 512:2560 + s * 512]
        nmr = R[0:1, 3072 + s * 512:3584 + s * 512]
        p.wait("vector", stat_tok)
        t_mu = TS(p, mu, g.ps[s][0:1, :], inv, None, ALU.mult)
        t_ms = TS(p, ms, g.ps[2 + s][0:1, :], inv, None, ALU.mult)
        p.wait("scalar", t_mu)
        t_mu2 = ACT(p, mu2, mu, AF.Square)
        p.wait("vector", t_mu2)
        t_var = TT(p, var, ms, mu2, ALU.subtract)
        p.wait("scalar", t_var)
        ACT(p, rstd, var, AF.Ln, bias=LN_EPS)
        t_rs = ACT(p, rstd, rstd, AF.Exp, scale=-0.5)
        p.wait("vector", t_rs)
        t_nmr = STT(p, nmr, mu, -1.0, rstd, ALU.mult, ALU.mult)
        p.wait("tensor", t_rs)
        p.wait("tensor", t_nmr)
        MM(p, g.ps[4 + s][:, :], g.ones[0:1, :], rstd, True, True, inc=False)
        tok = MM(p, g.ps[6 + s][:, :], g.ones[0:1, :], nmr, True, True)
    return tok


def ln_norm_chunk(p, g, out_b16, src, scr, W, bc_tok):
    tok = None
    for s in range(W // 512):
        p.wait("vector", bc_tok)
        TT(p, scr[:, 0:512], src[:, s * 512:(s + 1) * 512], g.ps[4 + s][:, :], ALU.mult)
        tok = TT(p, out_b16[:, s * 512:(s + 1) * 512], scr[:, 0:512],
                 g.ps[6 + s][:, :], ALU.add)
    return tok


def ln_resid_chunk(p, g, xnew, src, xr, scr, W, bc_tok):
    tok = None
    for s in range(W // 512):
        p.wait("vector", bc_tok)
        TT(p, scr[:, 0:512], src[:, s * 512:(s + 1) * 512], g.ps[4 + s][:, :], ALU.mult)
        TT(p, scr[:, 0:512], scr[:, 0:512], g.ps[6 + s][:, :], ALU.add)
        tok = TT(p, xnew[:, s * 512:(s + 1) * 512], scr[:, 0:512],
                 xr[:, s * 512:(s + 1) * 512], ALU.add)
    return tok


def w_rearr(dram_ap, col0, ncols):
    return dram_ap[:, col0:col0 + ncols].rearrange("(c p) o -> p c o", p=128)



def emit_s2(p, g):
    d = g.d
    nc = p.nc
    W = 1024
    NIC = 44
    with ExitStack() as es:
        xln = [es.enter_context(nc.sbuf_tensor(f"s2_xln{k}", [128, W], BF16))
               for k in range(NKC)]
        sqb = es.enter_context(nc.sbuf_tensor("s2_sq", [128, W], BF16))
        h = [es.enter_context(nc.sbuf_tensor(f"s2_h{i}", [128, W], BF16))
             for i in range(NIC)]
        scr = es.enter_context(nc.sbuf_tensor("s2_scr", [128, 512], F32))
        hg = es.enter_context(nc.sbuf_tensor("s2_hg", [128, 512], F32))
        wg_b = [es.enter_context(nc.sbuf_tensor(f"s2_wg{i}", [128, 2048], BF16)) for i in range(2)]
        wu_b = [es.enter_context(nc.sbuf_tensor(f"s2_wu{i}", [128, 2048], BF16)) for i in range(2)]
        wd_b = [es.enter_context(nc.sbuf_tensor("s2_wd0", [128, NIC * 128], BF16))]
        gs = es.enter_context(nc.sbuf_tensor("s2_gs", [128, 44], F32))
        us = es.enter_context(nc.sbuf_tensor("s2_us", [128, 44], F32))
        ds = es.enter_context(nc.sbuf_tensor("s2_ds", [128, 16], F32))
        delta = xln
        xres = [es.enter_context(nc.sbuf_tensor(f"s2_xr{i}", [128, W], F32)) for i in range(2)]
        xnew = es.enter_context(nc.sbuf_tensor("s2_xn", [128, W], F32))

        p.ld(gs[:], d["w0_gs"][:])
        p.ld(us[:], d["w0_us"][:])
        p.ld(ds[:], d["w0_ds"][:])
        p.barrier()

        for tt in range(T // W):
            c0 = tt * W
            toks = [p.ca(xln[k][:], d["x1"][k * 128:(k + 1) * 128, c0:c0 + W])
                    for k in range(NKC)]
            st_tok = None
            for k in range(NKC):
                st_tok = ln_stats_chunk(p, g, xln[k], sqb, W, k == 0, k == NKC - 1,
                                        src_tok=toks[k], war=st_tok)
            bc = ln_rows_and_bcast(p, g, W, D, st_tok)
            for k in range(NKC):
                ln_norm_chunk(p, g, xln[k], xln[k], scr, W, bc)
            p.barrier()

            ev_toks = {}
            hg_guard = {}
            mm_toks = {}
            for ic in range(NIC):
                bsel = ic % 2
                if ic >= 2:
                    p.wait("gpsimd", mm_toks[ic - 2])
                wg_t = p.ca(wg_b[bsel][:].rearrange("p (c o) -> p c o", o=128),
                            w_rearr(d["w0_g"], ic * 128, 128))
                wu_t = p.ca(wu_b[bsel][:].rearrange("p (c o) -> p c o", o=128),
                            w_rearr(d["w0_u"], ic * 128, 128))
                bks = [g.ps[4 * bsel + j] for j in range(4)]
                if ic >= 2:
                    p.wait("tensor", ev_toks[ic - 2])
                p.wait("tensor", wg_t)
                p.wait("tensor", wu_t)
                tok_u = None
                for s in range(2):
                    mm_chain(p, bks[s][:, :],
                             [(wg_b[bsel][:, kc * 128:(kc + 1) * 128],
                               xln[kc][:, s * 512:(s + 1) * 512]) for kc in range(NKC)],
                             inc=False)
                    tok_u = mm_chain(p, bks[2 + s][:, :],
                                     [(wu_b[bsel][:, kc * 128:(kc + 1) * 128],
                                       xln[kc][:, s * 512:(s + 1) * 512]) for kc in range(NKC)])
                last_tok = hg_guard.get("tok")
                for s in range(2):
                    p.wait("scalar", tok_u)
                    if last_tok is not None:
                        p.wait("scalar", last_tok)
                    t_hg = p.ac(lambda e, o=hg[:, 0:512], i=bks[s][:, :], sc=gs[:, ic:ic + 1]:
                                e.activation(o, i, AF.Silu, scale=sc))
                    p.wait("vector", t_hg)
                    last_tok = STT(p, h[ic][:, s * 512:(s + 1) * 512], bks[2 + s][:, :],
                                   us[:, ic:ic + 1], hg[:, 0:512], ALU.mult, ALU.mult)
                ev_toks[ic] = last_tok
                hg_guard["tok"] = last_tok
                mm_toks[ic] = tok_u
            p.barrier()

            dn_toks = {}
            for oc in range(NKC):
                bsel = 0
                if oc >= 1:
                    p.wait("gpsimd", prev_mm_tok)
                wd_t = p.ca(wd_b[bsel][:].rearrange("p (c o) -> p c o", o=128),
                            w_rearr(d["w0_d"], oc * 128, 128))
                bks = [g.ps[2 * (oc % 2) + j] for j in range(2)]
                if oc >= 2:
                    p.wait("tensor", dn_toks[oc - 2])
                p.wait("tensor", wd_t)
                toks_d = []
                for s in range(2):
                    toks_d.append(mm_chain(p, bks[s][:, :],
                                           [(wd_b[bsel][:, kc * 128:(kc + 1) * 128],
                                             h[kc][:, s * 512:(s + 1) * 512]) for kc in range(NIC)]))
                prev_mm_tok = toks_d[1]
                t_ev = None
                for s in range(2):
                    p.wait("scalar", toks_d[s])
                    t_ev = p.ac(lambda e, o=delta[oc][:, s * 512:(s + 1) * 512],
                                i=bks[s][:, :], sc=ds[:, oc:oc + 1]:
                                e.activation(o, i, AF.Copy, scale=sc))
                dn_toks[oc] = t_ev
            p.barrier()

            if DEBUG_TAP == "s2_delta":
                for k in range(NKC):
                    p.st(d["t_dbg"][k * 128:(k + 1) * 128, c0:c0 + W], delta[k][:])
                p.barrier()

            st_tok = None
            for k in range(NKC):
                st_tok = ln_stats_chunk(p, g, delta[k], sqb, W, k == 0, k == NKC - 1,
                                        war=st_tok)
            bc = ln_rows_and_bcast(p, g, W, D, st_tok)
            for k in range(NKC):
                xr = xres[k % 2]
                t_x = p.ld(xr[:], d["x1"][k * 128:(k + 1) * 128, c0:c0 + W])
                p.wait("vector", t_x)
                t_n = ln_resid_chunk(p, g, xnew, delta[k], xr, scr, W, bc)
                p.wait("sync", t_n)
                t_st = p.st(d["x2"][k * 128:(k + 1) * 128, c0:c0 + W], xnew[:])
                p.wait("vector", t_st)
            p.barrier()



def emit_s1(p, g):
    d = g.d
    nc = p.nc
    W = 1024
    NH = 16
    with ExitStack() as es:
        xln = [es.enter_context(nc.sbuf_tensor(f"s1_xln{k}", [128, W], BF16))
               for k in range(NKC)]
        sqb = es.enter_context(nc.sbuf_tensor("s1_sq", [128, W], BF16))
        scr = es.enter_context(nc.sbuf_tensor("s1_scr", [128, 512], F32))
        y = [es.enter_context(nc.sbuf_tensor(f"s1_y{h}", [128, W], BF16))
             for h in range(NH)]
        delta = [es.enter_context(nc.sbuf_tensor(f"s1_d{k}", [128, W], BF16))
                 for k in range(NKC)]
        qf = es.enter_context(nc.sbuf_tensor("s1_qf", [128, W], F32))
        kf = es.enter_context(nc.sbuf_tensor("s1_kf", [128, W], F32))
        vf = es.enter_context(nc.sbuf_tensor("s1_vf", [128, W], F32))
        af = es.enter_context(nc.sbuf_tensor("s1_af", [128, W], F32))
        gf = es.enter_context(nc.sbuf_tensor("s1_gf", [128, W], F32))
        w1 = es.enter_context(nc.sbuf_tensor("s1_w1", [128, W], F32))
        w2 = es.enter_context(nc.sbuf_tensor("s1_w2", [128, W], F32))
        w3 = es.enter_context(nc.sbuf_tensor("s1_w3", [128, W], F32))
        wb = [es.enter_context(nc.sbuf_tensor(f"s1_wb{i}", [128, 2048], BF16))
              for i in range(2)]
        wo = [es.enter_context(nc.sbuf_tensor(f"s1_wo{i}", [128, 2048], BF16))
              for i in range(2)]
        stc = es.enter_context(nc.sbuf_tensor("s1_stc", [128, NH], F32))
        xres = [es.enter_context(nc.sbuf_tensor(f"s1_xr{i}", [128, W], F32)) for i in range(2)]
        xnew = es.enter_context(nc.sbuf_tensor("s1_xn", [128, W], F32))
        p.barrier()

        for tt in range(T // W):
            c0 = tt * W
            toks = [p.ca(xln[k][:], d["x0"][k * 128:(k + 1) * 128, c0:c0 + W])
                    for k in range(NKC)]
            st_tok = None
            for k in range(NKC):
                st_tok = ln_stats_chunk(p, g, xln[k], sqb, W, k == 0, k == NKC - 1,
                                        src_tok=toks[k], war=st_tok)
            bc = ln_rows_and_bcast(p, g, W, D, st_tok)
            for k in range(NKC):
                ln_norm_chunk(p, g, xln[k], xln[k], scr, W, bc)
            p.barrier()

            wtoks = {}
            evg = {}
            rows_guard = None
            st45_guard = None
            bc67_guard = None
            prev_ty = None
            for h in range(NH):
                evac = {}
                if prev_ty is not None:
                    p.wait("scalar", prev_ty)
                for isec, sec in enumerate(("q", "k", "v", "a", "g")):
                    gi = h * 5 + isec
                    bsel = gi % 2
                    if gi >= 2:
                        p.wait("gpsimd", wtoks[gi - 2])
                    w_t = p.ca(wb[bsel][:].rearrange("p (c o) -> p c o", o=128),
                               w_rearr(d["w0_in"], isec * 2048 + h * 128, 128))
                    pb = [g.ps[2 * bsel + j] for j in range(2)]
                    if gi >= 2:
                        p.wait("tensor", evg[gi - 2])
                    p.wait("tensor", w_t)
                    toks_m = []
                    for s in range(2):
                        toks_m.append(mm_chain(p, pb[s][:, :],
                                               [(wb[bsel][:, kc * 128:(kc + 1) * 128],
                                                 xln[kc][:, s * 512:(s + 1) * 512])
                                                for kc in range(NKC)]))
                    wtoks[gi] = toks_m[1]
                    dst = {"q": qf, "k": kf, "v": vf, "a": af, "g": gf}[sec]
                    t_e = None
                    for s in range(2):
                        p.wait("scalar", toks_m[s])
                        if sec == "k":
                            t_e = ACT(p, dst[:, s * 512:(s + 1) * 512], pb[s][:, :], AF.Tanh)
                        elif sec == "a":
                            t_e = ACT(p, dst[:, s * 512:(s + 1) * 512], pb[s][:, :],
                                      AF.Sigmoid, bias=2.0)
                        elif sec == "g":
                            t_e = ACT(p, dst[:, s * 512:(s + 1) * 512], pb[s][:, :], AF.Sigmoid)
                        else:
                            t_e = ACT(p, dst[:, s * 512:(s + 1) * 512], pb[s][:, :], AF.Copy)
                    evac[sec] = t_e
                    evg[gi] = t_e
                p.wait("vector", evac["a"])
                t_a = TS(p, af[:], af[:], 0.6, 0.9995, ALU.max, ALU.min)
                p.wait("vector", evac["k"])
                p.wait("vector", evac["v"])
                TT(p, w1[:], kf[:], vf[:], ALU.mult)
                TT(p, w2[:], af[:], g.m0_64[:, 0:W], ALU.mult)
                TT(p, w3[:], af[:], g.m1_64[:, 0:W], ALU.mult)
                SCAN(p, w2[:], w2[:], w3[:], 1.0, ALU.mult, ALU.add)
                TS(p, w3[:], w2[:], 1e6, 1.0, ALU.mult, ALU.min)
                TT(p, w1[:], w1[:], w3[:], ALU.mult)
                init = 0.0 if tt == 0 else stc[:, h:h + 1]
                t_sc = SCAN(p, w2[:], af[:], w1[:], init, ALU.mult, ALU.add)
                CP(p, stc[:, h:h + 1], w2[:, W - 1:W])
                p.wait("tensor", evac["q"])
                if st45_guard is not None:
                    p.wait("tensor", st45_guard)
                qs_toks = []
                for s in range(2):
                    if s == 1:
                        p.wait("scalar", qs_toks[0])
                    t_sq2 = p.ac(lambda e, o=sqb[:, 0:512], i=qf[:, s * 512:(s + 1) * 512]:
                                 e.activation(o, i, AF.Square))
                    p.wait("tensor", t_sq2)
                    qs_toks.append(MM(p, g.ps[4 + s][0:1, :], g.onesb[:, 0:1],
                                      sqb[:, 0:512], True, True))
                if rows_guard is not None:
                    p.wait("vector", rows_guard)
                qr0 = g.rows[0:1, 0:512]
                qr1 = g.rows[0:1, 512:1024]
                p.wait("vector", qs_toks[0])
                p.wait("vector", qs_toks[1])
                TS(p, qr0, g.ps[4][0:1, :], 1.0 / 128, None, ALU.mult)
                t_r = TS(p, qr1, g.ps[5][0:1, :], 1.0 / 128, None, ALU.mult)
                p.wait("scalar", t_r)
                ACT(p, qr0, qr0, AF.Ln, bias=1.1920928955078125e-07)
                ACT(p, qr1, qr1, AF.Ln, bias=1.1920928955078125e-07)
                ACT(p, qr0, qr0, AF.Exp, scale=-0.5)
                t_r2 = ACT(p, qr1, qr1, AF.Exp, scale=-0.5)
                if bc67_guard is not None:
                    p.wait("tensor", bc67_guard)
                p.wait("tensor", t_r2)
                MM(p, g.ps[6][:, :], g.ones[0:1, :], qr0, True, True, inc=False)
                t_bc = MM(p, g.ps[7][:, :], g.ones[0:1, :], qr1, True, True)
                rows_guard = t_bc
                p.wait("vector", t_bc)
                t1a = TT(p, w1[:, 0:512], qf[:, 0:512], g.ps[6][:, :], ALU.mult)
                t1b = TT(p, w1[:, 512:W], qf[:, 512:W], g.ps[7][:, :], ALU.mult)
                st45_guard = t1b
                bc67_guard = t1b
                TT(p, w1[:], w1[:], w2[:], ALU.mult)
                TT(p, w1[:], w1[:], vf[:], ALU.subtract)
                TT(p, w1[:], w1[:], gf[:], ALU.mult)
                t_y = TT(p, y[h][:], w1[:], vf[:], ALU.add)
                prev_ty = t_y
            p.barrier()

            if DEBUG_TAP == "s1_state":
                p.st(d["t_dbg"].bitcast(F32)[0:128, tt * 16:(tt + 1) * 16], stc[:])
                p.barrier()

            dn_toks = {}
            prev_mm = None
            for oc in range(NKC):
                bsel = oc % 2
                if oc >= 2:
                    p.wait("gpsimd", prev_mm)
                wo_t = p.ca(wo[bsel][:].rearrange("p (c o) -> p c o", o=128),
                            w_rearr(d["w0_out"], oc * 128, 128))
                bks = [g.ps[2 * bsel + j] for j in range(2)]
                if oc >= 2:
                    p.wait("tensor", dn_toks[oc - 2])
                p.wait("tensor", wo_t)
                toks_d = []
                for s in range(2):
                    toks_d.append(mm_chain(p, bks[s][:, :],
                                           [(wo[bsel][:, kc * 128:(kc + 1) * 128],
                                             y[kc][:, s * 512:(s + 1) * 512])
                                            for kc in range(NKC)]))
                prev_mm = toks_d[1]
                t_ev = None
                for s in range(2):
                    p.wait("scalar", toks_d[s])
                    t_ev = ACT(p, delta[oc][:, s * 512:(s + 1) * 512], bks[s][:, :], AF.Copy)
                dn_toks[oc] = t_ev
            p.barrier()

            st_tok = None
            for k in range(NKC):
                st_tok = ln_stats_chunk(p, g, delta[k], sqb, W, k == 0, k == NKC - 1,
                                        war=st_tok)
            bc = ln_rows_and_bcast(p, g, W, D, st_tok)
            for k in range(NKC):
                xr = xres[k % 2]
                t_x = p.ld(xr[:], d["x0"][k * 128:(k + 1) * 128, c0:c0 + W])
                p.wait("vector", t_x)
                t_n = ln_resid_chunk(p, g, xnew, delta[k], xr, scr, W, bc)
                p.wait("sync", t_n)
                t_st = p.st(d["x1"][k * 128:(k + 1) * 128, c0:c0 + W], xnew[:])
                p.wait("vector", t_st)
            p.barrier()



def emit_s3(p, g):
    d = g.d
    nc = p.nc
    W = 1024
    with ExitStack() as es:
        xln = [es.enter_context(nc.sbuf_tensor(f"s3_xln{k}", [128, W], BF16))
               for k in range(NKC)]
        sqb = es.enter_context(nc.sbuf_tensor("s3_sq", [128, W], BF16))
        scr = es.enter_context(nc.sbuf_tensor("s3_scr", [128, 512], F32))
        wz = [es.enter_context(nc.sbuf_tensor(f"s3_wz{i}", [128, NKC * 512], BF16))
              for i in range(2)]
        wx = [es.enter_context(nc.sbuf_tensor(f"s3_wx{i}", [128, 2048], BF16))
              for i in range(2)]
        wdt = es.enter_context(nc.sbuf_tensor("s3_wdt", [128, NKC * 64], BF16))
        zst = [es.enter_context(nc.sbuf_tensor(f"s3_zst{i}", [128, 512], BF16))
               for i in range(2)]
        xst = [es.enter_context(nc.sbuf_tensor(f"s3_xst{i}", [128, W], BF16))
               for i in range(2)]
        dst = [es.enter_context(nc.sbuf_tensor(f"s3_dst{i}", [64, W], F32))
               for i in range(2)]
        zzt = es.enter_context(nc.sbuf_tensor("s3_zzt", [128, 4], BF16))
        p.barrier()
        t_z = p.dv(lambda e, o=zzt[:]: e.memset(o, 0.0))
        p.wait("sync", t_z)
        for oc in range(48):
            p.st(d["xbc"][oc * 128:(oc + 1) * 128, 0:3], zzt[:, 0:3])
        p.barrier()

        for tt in range(T // W):
            c0 = tt * W
            toks = [p.ca(xln[k][:], d["x2"][k * 128:(k + 1) * 128, c0:c0 + W])
                    for k in range(NKC)]
            st_tok = None
            for k in range(NKC):
                st_tok = ln_stats_chunk(p, g, xln[k], sqb, W, k == 0, k == NKC - 1,
                                        src_tok=toks[k], war=st_tok)
            bc = ln_rows_and_bcast(p, g, W, D, st_tok)
            for k in range(NKC):
                ln_norm_chunk(p, g, xln[k], xln[k], scr, W, bc)
            p.barrier()

            ev = {}
            wtok_prev = None
            st_prev = {}
            for zc in range(8):
                bsel = zc % 2
                if zc >= 2:
                    p.wait("gpsimd", wtok_prev)
                w_t = p.ca(wz[bsel][:].rearrange("p (c o) -> p c o", o=512),
                           w_rearr(d["w1_in"], zc * 512, 512))
                for tch in range(8):
                    bk = g.ps[tch]
                    if zc >= 1:
                        p.wait("tensor", ev[tch])
                    if tch == 0:
                        p.wait("tensor", w_t)
                    t_mm = mm_chain(p, bk[:, :],
                                    [(xln[kc][:, tch * 128:(tch + 1) * 128],
                                      wz[bsel][:, kc * 512:(kc + 1) * 512])
                                     for kc in range(NKC)])
                    if tch == 7:
                        wtok_prev = t_mm
                    sb_ = zst[tch % 2]
                    p.wait("scalar", t_mm)
                    p.wait("scalar", st_prev.get(tch % 2))
                    t_e = ACT(p, sb_[:], bk[:, :], AF.Silu)
                    ev[tch] = t_e
                    p.wait("sync", t_e)
                    st_prev[tch % 2] = p.st(
                        d["zs"][c0 + tch * 128:c0 + (tch + 1) * 128,
                                zc * 512:(zc + 1) * 512], sb_[:])
            p.barrier()

            dn = {}
            prev_mm = None
            st_prev = {}
            for oc in range(48):
                bsel = oc % 2
                if oc >= 2:
                    p.wait("gpsimd", prev_mm)
                w_t = p.ca(wx[bsel][:].rearrange("p (c o) -> p c o", o=128),
                           w_rearr(d["w1_in"], 4096 + oc * 128, 128))
                bks = [g.ps[2 * bsel + j] for j in range(2)]
                if oc >= 2:
                    p.wait("tensor", dn[oc - 2])
                p.wait("tensor", w_t)
                toks_d = []
                for sx in range(2):
                    toks_d.append(mm_chain(p, bks[sx][:, :],
                                           [(wx[bsel][:, kc * 128:(kc + 1) * 128],
                                             xln[kc][:, sx * 512:(sx + 1) * 512])
                                            for kc in range(NKC)]))
                prev_mm = toks_d[1]
                sb_ = xst[oc % 2]
                t_e = None
                for sx in range(2):
                    p.wait("scalar", toks_d[sx])
                    p.wait("scalar", st_prev.get(oc % 2))
                    t_e = ACT(p, sb_[:, sx * 512:(sx + 1) * 512], bks[sx][:, :], AF.Copy)
                dn[oc] = t_e
                p.wait("sync", t_e)
                st_prev[oc % 2] = p.st(
                    d["xbc"][oc * 128:(oc + 1) * 128, 3 + c0:3 + c0 + W], sb_[:])
            p.barrier()

            p.barrier()
            w_t = p.ca(wdt[:].rearrange("p (c o) -> p c o", o=64),
                       w_rearr(d["w1_in"], 10240, 64))
            p.wait("tensor", w_t)
            for sx in range(2):
                bk = g.ps[sx]
                t_mm = mm_chain(p, bk[0:64, :],
                                [(wdt[:, kc * 64:(kc + 1) * 64],
                                  xln[kc][:, sx * 512:(sx + 1) * 512])
                                 for kc in range(NKC)])
                p.wait("scalar", t_mm)
                t_e = ACT(p, dst[0][0:64, sx * 512:(sx + 1) * 512], bk[0:64, :], AF.Copy)
            p.wait("sync", t_e)
            p.st(d["dtr"][0:64, c0:c0 + W], dst[0][0:64, :])
            p.barrier()



def emit_s5a(p, g):
    d = g.d
    nc = p.nc
    W = 1024
    with ExitStack() as es:
        xln = [es.enter_context(nc.sbuf_tensor(f"s5_xln{k}", [128, W], BF16))
               for k in range(NKC)]
        sqb = es.enter_context(nc.sbuf_tensor("s5_sq", [128, W], BF16))
        scr = es.enter_context(nc.sbuf_tensor("s5_scr", [128, 512], F32))
        hg = es.enter_context(nc.sbuf_tensor("s5_hg", [128, 512], F32))
        wgr = es.enter_context(nc.sbuf_tensor("s5_wgr", [128, NKC * 8], BF16))
        bgr = es.enter_context(nc.sbuf_tensor("s5_bgr", [1, 8], F32))
        bgbc = es.enter_context(nc.sbuf_tensor("s5_bgbc", [128, 8], F32))
        sel = es.enter_context(nc.sbuf_tensor("s5_sel", [8, 1024], F32))
        wfm_sb = es.enter_context(nc.sbuf_tensor("s5_wfm", [8, W], F32))
        wbc = es.enter_context(nc.sbuf_tensor("s5_wbc", [128, W], F32))
        wr = [es.enter_context(nc.sbuf_tensor(f"s5_wr{i}", [128, 8], F32))
              for i in range(4)]
        cc = es.enter_context(nc.sbuf_tensor("s5_cc", [128, 8], F32))
        wg2 = [es.enter_context(nc.sbuf_tensor(f"s5_wg{i}", [128, 2048], BF16))
               for i in range(2)]
        wu2 = [es.enter_context(nc.sbuf_tensor(f"s5_wu{i}", [128, 2048], BF16))
               for i in range(2)]
        bgc = es.enter_context(nc.sbuf_tensor("s5_bgc", [128, 48], F32))
        buc = es.enter_context(nc.sbuf_tensor("s5_buc", [128, 48], F32))
        bsuc = es.enter_context(nc.sbuf_tensor("s5_bsuc", [128, 16], F32))
        hst = [es.enter_context(nc.sbuf_tensor(f"s5_hst{i}", [128, W], BF16))
               for i in range(2)]
        p.ld(bgr[:], d["w2_bg"][:])
        p.ld(sel[:], d["c_sel"][:])
        p.ld(bgc[:], d["w2_bgate"][:])
        p.ld(buc[:], d["w2_bup"][:])
        p.ld(bsuc[:], d["w2_bsu"][:])
        t_w = p.ca(wgr[:].rearrange("p (c o) -> p c o", o=8),
                   w_rearr(d["w2_wg"], 0, 8))
        p.barrier()
        t_mm = MM(p, g.ps[0][:, 0:8], g.ones[0:1, :], bgr[0:1, :], True, True)
        p.wait("vector", t_mm)
        t_cp = CP(p, bgbc[:], g.ps[0][:, 0:8])
        p.barrier()

        for tt in range(T // W):
            c0 = tt * W
            toks = [p.ca(xln[k][:], d["x3"][k * 128:(k + 1) * 128, c0:c0 + W])
                    for k in range(NKC)]
            st_tok = None
            for k in range(NKC):
                st_tok = ln_stats_chunk(p, g, xln[k], sqb, W, k == 0, k == NKC - 1,
                                        src_tok=toks[k], war=st_tok)
            bc = ln_rows_and_bcast(p, g, W, D, st_tok)
            for k in range(NKC):
                ln_norm_chunk(p, g, xln[k], xln[k], scr, W, bc)
            p.barrier()

            do_router = "router" in S5A_PHASES
            tr_guard = None
            rt_guard = {}
            wr_guard = None
            for tch in range(8 if do_router else 0):
                bk = g.ps[tch % 2]
                if tch >= 2:
                    p.wait("tensor", rt_guard[tch - 2])
                t_mm = mm_chain(p, bk[:, 0:8],
                                [(xln[kc][:, tch * 128:(tch + 1) * 128],
                                  wgr[:, kc * 8:(kc + 1) * 8]) for kc in range(NKC)])
                p.wait("vector", t_mm)
                if wr_guard is not None:
                    p.wait("vector", wr_guard)
                t_tt0 = TT(p, wr[0][:], bk[:, 0:8], bgbc[:], ALU.add)
                rt_guard[tch] = t_tt0
                RED(p, cc[:, 0:1], wr[0][:], ALU.min)
                TS(p, wr[1][:], wr[0][:], cc[:, 0:1], None, ALU.is_equal)
                STT(p, wr[2][:], wr[1][:], 1e6, wr[0][:], ALU.mult, ALU.add)
                RED(p, cc[:, 1:2], wr[2][:], ALU.min)
                TS(p, wr[1][:], wr[0][:], cc[:, 1:2], None, ALU.is_gt)
                RED(p, cc[:, 2:3], wr[0][:], ALU.max)
                t_n = TS(p, cc[:, 3:4], cc[:, 2:3], -1.0, None, ALU.mult)
                p.wait("scalar", t_n)
                t_ex = p.ac(lambda e, o=wr[2][:], i=wr[0][:], b=cc[:, 3:4]:
                            e.activation(o, i, AF.Exp, bias=b))
                p.wait("vector", t_ex)
                TT(p, wr[3][:], wr[2][:], wr[1][:], ALU.mult)
                RED(p, cc[:, 4:5], wr[3][:], ALU.add)
                t_rz = p.dv(lambda e, o=cc[:, 5:6], i=cc[:, 4:5]: e.reciprocal(o, i))
                p.wait("vector", t_rz)
                t_wtm = TS(p, wr[0][:], wr[3][:], cc[:, 5:6], None, ALU.mult)
                if tr_guard is not None:
                    p.wait("tensor", tr_guard)
                p.wait("tensor", t_wtm)
                t_tr = TRN(p, g.ps[7][0:8, 0:128], wr[0][:], g.idn[:])
                wr_guard = t_tr
                p.wait("vector", t_tr)
                tr_guard = CP(p, wfm_sb[0:8, tch * 128:(tch + 1) * 128],
                              g.ps[7][0:8, 0:128])
            if do_router:
                p.wait("sync", tr_guard)
                p.st(d["wfm"][:, c0:c0 + W], wfm_sb[:])
            p.barrier()

            ev = {}
            ev_mm = {}
            hst_guard = {}
            gi = 0
            for e_ in range(6 if "experts" in S5A_PHASES else 0):
                if gi >= 1:
                    p.wait("tensor", ev[gi - 1])
                if gi >= 2:
                    p.wait("tensor", ev[gi - 2])
                t_bc = None
                for sx in range(2):
                    if t_bc is not None:
                        p.wait("tensor", t_bc)
                    t_mm = MM(p, g.ps[6][:, :], sel[0:8, e_ * 128:(e_ + 1) * 128],
                              wfm_sb[0:8, sx * 512:(sx + 1) * 512], True, True)
                    p.wait("vector", t_mm)
                    t_bc = CP(p, wbc[:, sx * 512:(sx + 1) * 512], g.ps[6][:, :])
                prev_mm = None
                for ic in range(8):
                    bsel = ic % 2
                    if gi >= 2:
                        p.wait("gpsimd", ev_mm[gi - 2])
                    wg_t = p.ca(wg2[bsel][:].rearrange("p (c o) -> p c o", o=128),
                                w_rearr(d["w2_wgate"][e_ * 2048:(e_ + 1) * 2048, :],
                                        ic * 128, 128))
                    wu_t = p.ca(wu2[bsel][:].rearrange("p (c o) -> p c o", o=128),
                                w_rearr(d["w2_wup"][e_ * 2048:(e_ + 1) * 2048, :],
                                        ic * 128, 128))
                    bks = [g.ps[4 * bsel + j] for j in range(4)]
                    if gi >= 2:
                        p.wait("tensor", ev[gi - 2])
                    if bsel == 1:
                        p.wait("tensor", t_bc)
                    p.wait("tensor", wg_t)
                    p.wait("tensor", wu_t)
                    tok_u = None
                    for sx in range(2):
                        mm_chain(p, bks[sx][:, :],
                                 [(wg2[bsel][:, kc * 128:(kc + 1) * 128],
                                   xln[kc][:, sx * 512:(sx + 1) * 512])
                                  for kc in range(NKC)], inc=False)
                        tok_u = mm_chain(p, bks[2 + sx][:, :],
                                         [(wu2[bsel][:, kc * 128:(kc + 1) * 128],
                                           xln[kc][:, sx * 512:(sx + 1) * 512])
                                          for kc in range(NKC)])
                    last = ev.get(gi - 1)
                    t_e = None
                    for sx in range(2):
                        p.wait("scalar", tok_u)
                        if t_e is not None or last is not None:
                            p.wait("scalar", t_e if t_e is not None else last)
                        t_hg = p.ac(lambda e, o=hg[:], i=bks[sx][:, :],
                                    b=bgc[:, e_ * 8 + ic:e_ * 8 + ic + 1]:
                                    e.activation(o, i, AF.Silu, bias=b))
                        p.wait("vector", t_hg)
                        p.wait("vector", t_bc)
                        p.wait("vector", hst_guard.get(ic % 2))
                        STT(p, scr[:], bks[2 + sx][:, :],
                            buc[:, e_ * 8 + ic:e_ * 8 + ic + 1], hg[:], ALU.add, ALU.mult)
                        t_e = TT(p, hst[ic % 2][:, sx * 512:(sx + 1) * 512], scr[:],
                                 wbc[:, sx * 512:(sx + 1) * 512], ALU.mult)
                    ev[gi] = t_e
                    ev_mm[gi] = tok_u
                    p.wait("sync", t_e)
                    hst_guard[ic % 2] = p.st(
                        d["hxp"][e_ * 1024 + ic * 128:e_ * 1024 + (ic + 1) * 128,
                                 c0:c0 + W], hst[ic % 2][:])
                    gi += 1
            p.barrier()

            ev = {}
            sh_guard = {}
            prev_mm = None
            for ic in range(16 if "shared" in S5A_PHASES else 0):
                bsel = ic % 2
                if ic >= 2:
                    p.wait("gpsimd", prev_mm)
                wg_t = p.ca(wg2[bsel][:].rearrange("p (c o) -> p c o", o=128),
                            w_rearr(d["w2_wsu"], ic * 128, 128))
                bks = [g.ps[2 * bsel + j] for j in range(2)]
                if ic >= 2:
                    p.wait("tensor", ev[ic - 2])
                p.wait("tensor", wg_t)
                toks_d = []
                for sx in range(2):
                    toks_d.append(mm_chain(p, bks[sx][:, :],
                                           [(wg2[bsel][:, kc * 128:(kc + 1) * 128],
                                             xln[kc][:, sx * 512:(sx + 1) * 512])
                                            for kc in range(NKC)]))
                prev_mm = toks_d[1]
                t_e = None
                for sx in range(2):
                    p.wait("scalar", toks_d[sx])
                    p.wait("scalar", ev.get(ic - 1))
                    p.wait("scalar", sh_guard.get(ic % 2))
                    t_e = ACT(p, hst[ic % 2][:, sx * 512:(sx + 1) * 512], bks[sx][:, :],
                              AF.Silu, bias=bsuc[:, ic:ic + 1])
                ev[ic] = t_e
                p.wait("sync", t_e)
                sh_guard[ic % 2] = p.st(
                    d["hxp"][6144 + ic * 128:6144 + (ic + 1) * 128, c0:c0 + W],
                    hst[ic % 2][:])
            p.barrier()



def emit_s5b(p, g):
    d = g.d
    nc = p.nc
    W = 512
    with ExitStack() as es:
        hsb = es.enter_context(nc.sbuf_tensor("s5b_h", [128, 64 * W], BF16))
        wdb = [es.enter_context(nc.sbuf_tensor(f"s5b_wd{i}", [128, 64 * 128], BF16))
               for i in range(2)]
        delta = [es.enter_context(nc.sbuf_tensor(f"s5b_d{k}", [128, W], BF16))
                 for k in range(NKC)]
        wfm5 = es.enter_context(nc.sbuf_tensor("s5b_wfm", [8, W], F32))
        bdt = es.enter_context(nc.sbuf_tensor("s5b_bdt", [6, D], F32))
        bsdc = es.enter_context(nc.sbuf_tensor("s5b_bsdc", [128, 16], F32))
        sqb = es.enter_context(nc.sbuf_tensor("s5b_sq", [128, W], BF16))
        scr = es.enter_context(nc.sbuf_tensor("s5b_scr", [128, 512], F32))
        xres = [es.enter_context(nc.sbuf_tensor(f"s5b_xr{i}", [128, W], F32))
                for i in range(2)]
        xnew = es.enter_context(nc.sbuf_tensor("s5b_xn", [128, W], F32))
        p.ld(bdt[:], d["w2_bdown"][:])
        p.ld(bsdc[:], d["w2_bsd"][:])
        p.barrier()

        for tt in range(T // W):
            c0 = tt * W
            t_h = p.ld(hsb[:].rearrange("p (r t) -> p r t", t=W),
                       d["hxp"][:, c0:c0 + W].rearrange("(r p) t -> p r t", p=128))
            t_w5 = p.ld(wfm5[:], d["wfm"][:, c0:c0 + W])
            dn = {}
            prev_mm = None
            for oc in range(NKC):
                bsel = oc % 2
                if oc >= 2:
                    p.wait("gpsimd", prev_mm)
                w_t = p.ca(wdb[bsel][:, 0:48 * 128].rearrange("p (c o) -> p c o", o=128),
                           w_rearr(d["w2_wdown"], oc * 128, 128))
                w_t2 = p.ca(wdb[bsel][:, 48 * 128:64 * 128].rearrange("p (c o) -> p c o", o=128),
                            w_rearr(d["w2_wsd"], oc * 128, 128))
                bk = g.ps[oc % 4]
                if oc >= 4:
                    p.wait("tensor", dn[oc - 4])
                p.wait("tensor", w_t)
                p.wait("tensor", w_t2)
                if oc == 0:
                    p.wait("tensor", t_h)
                    p.wait("tensor", t_w5)
                pairs = [(wdb[bsel][:, c * 128:(c + 1) * 128],
                          hsb[:, c * W:(c + 1) * W]) for c in range(64)]
                for i, (l, r) in enumerate(pairs):
                    MM(p, bk[:, :], l, r, i == 0, False, inc=False)
                t_mm = MM(p, bk[:, :], bdt[0:6, oc * 128:(oc + 1) * 128],
                          wfm5[0:6, :], False, True)
                prev_mm = t_mm
                p.wait("vector", t_mm)
                t_e = TS(p, delta[oc][:], bk[:, :], bsdc[:, oc:oc + 1], None, ALU.add)
                dn[oc] = t_e
            p.barrier()

            st_tok = None
            for k in range(NKC):
                st_tok = ln_stats_chunk(p, g, delta[k], sqb, W, k == 0, k == NKC - 1,
                                        war=st_tok)
            bc = ln_rows_and_bcast(p, g, W, D, st_tok)
            for k in range(NKC):
                xr = xres[k % 2]
                t_x = p.ld(xr[:], d["x3"][k * 128:(k + 1) * 128, c0:c0 + W])
                p.wait("vector", t_x)
                t_n = ln_resid_chunk(p, g, xnew, delta[k], xr, scr, W, bc)
                p.wait("sync", t_n)
                t_st = p.st(d["out"][k * 128:(k + 1) * 128, c0:c0 + W], xnew[:])
                p.wait("vector", t_st)
            p.barrier()



def emit_s4(p, g):
    d = g.d
    nc = p.nc
    W = 512
    NCH = W // 128
    with ExitStack() as es:
        sb = lambda n, shp, dt=F32: es.enter_context(nc.sbuf_tensor(n, shp, dt))
        x_sb = sb("s4_x", [128, 8 * W], BF16)
        b_sb = sb("s4_b", [128, 8 * W], BF16)
        c_sb = sb("s4_c", [128, 32 * W], BF16)
        xin = [sb(f"s4_xin{i}", [128, W + 3], BF16) for i in range(2)]
        cv = [sb(f"s4_cv{i}", [128, W]) for i in range(3)]
        cwt = sb("s4_cw", [128, 192])
        cbt = sb("s4_cb", [128, 48])
        dtb = sb("s4_dtb", [64, 1])
        alog = sb("s4_alog", [64, 1])
        nA = sb("s4_nA", [64, 1])
        d64 = sb("s4_d64", [1, 64])
        dcol = sb("s4_dcol", [128, 64])
        dtv = sb("s4_dtv", [64, W])
        lndt = sb("s4_lndt", [64, W])
        dAf = sb("s4_dA", [64, W])
        cum = sb("s4_cum", [64, W])
        rvc = sb("s4_rvc", [64, W])
        d2e = sb("s4_d2e", [64, W])
        eend = sb("s4_eend", [64, NCH])
        erow = sb("s4_erow", [1, 64])
        hdecp = sb("s4_hdecp", [128, 32])
        tmA = sb("s4_tmA", [128, 64])
        tmL = sb("s4_tmL", [128, 64])
        tmE = sb("s4_tmE", [128, 64])
        tmD = sb("s4_tmD", [128, 64])
        x_tm = sb("s4_xtm", [128, 16 * 64], BF16)
        b_tm = sb("s4_btm", [128, 16 * 64], BF16)
        bdup = sb("s4_bdup", [128, 16 * 128], BF16)
        Ht = sb("s4_H", [128, 2048])
        hb16 = sb("s4_hb16", [128, 128], BF16)
        lda4 = sb("s4_lda4", [128, 512], BF16)
        mexp4 = sb("s4_mexp4", [128, 512], BF16)
        gma = sb("s4_gma", [128, 256], BF16)
        gmb = sb("s4_gmb", [128, 256], BF16)
        bd4 = sb("s4_bd4", [128, 256], BF16)
        w1t = sb("s4_w1t", [128, 256])
        w2t = sb("s4_w2t", [128, 256])
        w3a = sb("s4_w3a", [128, 128])
        w3b = sb("s4_w3b", [128, 128])
        hmt = sb("s4_hmt", [128, 128])
        Yt = sb("s4_Y", [128, 4096], BF16)
        ssq = sb("s4_ssq", [128, 64])
        rstd = sb("s4_rstd", [128, 64])
        zt = [sb("s4_z0", [128, 4096], BF16)]
        nrm = sb("s4_nrm", [128, 32 * W], BF16)
        wob = [sb(f"s4_wob{i}", [128, 32 * 128], BF16) for i in range(2)]
        wtmp = wob[0][:, 0:2048].rearrange("p n -> p n") if False else None
        nwsb = sb("s4_nw", [128, 32])
        delta = [x_sb[:, k * W:(k + 1) * W] for k in range(8)] + \
                [b_sb[:, (k - 8) * W:(k - 7) * W] for k in range(8, 16)]
        sqb = sb("s4_sqb", [128, W], BF16)
        scr = sb("s4_scr", [128, 512])
        xres = [sb(f"s4_xr{i}", [128, W]) for i in range(2)]
        xnew = sb("s4_xn", [128, W])
        mtop = sb("s4_mtop", [1, 128])
        mbot = sb("s4_mbot", [1, 128])

        p.ld(cwt[:], d["w1_cw"][:])
        p.ld(cbt[:], d["w1_cb"][:])
        p.ld(dtb[:], d["w1_dtb"][:])
        p.ld(alog[:], d["w1_alog"][:])
        p.ld(d64[:], d["w1_d64"][:])
        p.ld(nwsb[:], d["w1_nw"][:])
        p.ld(mtop[:], d["c_mtop"][:])
        t_pl = p.ld(mbot[:], d["c_mbot"][:])
        p.barrier()
        t_a = ACT(p, nA[:], alog[:], AF.Exp)
        p.wait("vector", t_a)
        TS(p, nA[:], nA[:], -1.0, None, ALU.mult)
        t_mm = MM(p, g.ps[0][:, 0:64], g.ones[0:1, :], d64[0:1, :], True, True)
        p.wait("vector", t_mm)
        CP(p, dcol[:], g.ps[0][:, 0:64])
        p.dv(lambda e, o=Ht[:]: e.memset(o, 0.0))
        pw_g = {}
        for kc in range(32):
            if kc >= 1:
                p.wait("gpsimd", pw_g[kc - 1])
            t_w = p.ca(wob[0][:, 0:2048], d["w1_out"][kc * 128:(kc + 1) * 128, :])
            p.wait("vector", t_w)
            t_s = TS(p, wob[0][:, 0:2048], wob[0][:, 0:2048],
                     nwsb[:, kc:kc + 1], None, ALU.mult)
            p.wait("sync", t_s)
            pw_g[kc] = p.st(d["wout_bf"][kc * 128:(kc + 1) * 128, :], wob[0][:, 0:2048])
        p.barrier()

        zld_g = {}
        grp_dv = None
        grp_ac = None
        ch_dv = None
        ch_ac = None
        for tt in range(T // W):
            c0 = tt * W
            cv_g = None
            t_si = None
            for oc in range(48):
                bsel = oc % 2
                if cv_g is not None:
                    p.wait("sync", cv_g)
                t_x = p.ld(xin[bsel][:], d["xbc"][oc * 128:(oc + 1) * 128,
                                                  c0:c0 + W + 3])
                p.wait("vector", t_x)
                if t_si is not None:
                    p.wait("vector", t_si)
                TS(p, cv[0][:], xin[bsel][:, 0:W], cwt[:, oc * 4:oc * 4 + 1],
                   None, ALU.mult)
                STT(p, cv[1][:], xin[bsel][:, 1:W + 1], cwt[:, oc * 4 + 1:oc * 4 + 2],
                    cv[0][:], ALU.mult, ALU.add)
                STT(p, cv[2][:], xin[bsel][:, 2:W + 2], cwt[:, oc * 4 + 2:oc * 4 + 3],
                    cv[1][:], ALU.mult, ALU.add)
                t_c = STT(p, cv[0][:], xin[bsel][:, 3:W + 3], cwt[:, oc * 4 + 3:oc * 4 + 4],
                          cv[2][:], ALU.mult, ALU.add)
                cv_g = t_c
                p.wait("scalar", t_c)
                if oc < 8:
                    dst = x_sb[:, oc * W:(oc + 1) * W]
                elif oc < 16:
                    dst = b_sb[:, (oc - 8) * W:(oc - 7) * W]
                else:
                    dst = c_sb[:, (oc - 16) * W:(oc - 15) * W]
                t_si = ACT(p, dst, cv[0][:], AF.Silu, bias=cbt[:, oc:oc + 1])
            t_d = p.ld(dtv[0:64, :], d["dtr"][0:64, c0:c0 + W])
            p.wait("vector", t_d)
            p.wait("vector", t_si)
            TS(p, cv[0][0:64, :], dtv[0:64, :], dtb[:, 0:1], None, ALU.add)
            t_rl = TS(p, cv[1][0:64, :], cv[0][0:64, :], 0.0, None, ALU.max)
            p.wait("scalar", t_rl)
            ACT(p, cv[2][0:64, :], cv[0][0:64, :], AF.Abs)
            t_e1 = ACT(p, cv[2][0:64, :], cv[2][0:64, :], AF.Exp, scale=-1.0)
            t_e2 = ACT(p, cv[2][0:64, :], cv[2][0:64, :], AF.Ln, bias=1.0)
            p.wait("vector", t_e2)
            t_dtv = TT(p, dtv[0:64, :], cv[1][0:64, :], cv[2][0:64, :], ALU.add)
            p.wait("scalar", t_dtv)
            t_ln = ACT(p, lndt[0:64, :], dtv[0:64, :], AF.Ln)
            p.wait("vector", t_ln)
            TS(p, dAf[0:64, :], dtv[0:64, :], nA[:, 0:1], None, ALU.mult)
            SCAN(p, cum[0:64, :], g.mc128[0:64, 0:W], dAf[0:64, :], 0.0,
                 ALU.mult, ALU.add)
            SCAN(p, rvc[0:64, ::-1], g.mc128r[0:64, W - 1::-1], dAf[0:64, ::-1],
                 0.0, ALU.mult, ALU.add)
            TT(p, rvc[0:64, :], rvc[0:64, :], dAf[0:64, :], ALU.subtract)
            t_d2p = TT(p, d2e[0:64, :], rvc[0:64, :], lndt[0:64, :], ALU.add)
            p.wait("scalar", t_d2p)
            t_d2 = ACT(p, d2e[0:64, :], d2e[0:64, :], AF.Exp)
            t_ee = ACT(p, eend[0:64, :], cum[0:64, 127::128], AF.Exp)
            p.wait("vector", t_ee)
            p.barrier()
            if DEBUG_TAP == "s4" and tt == 0:
                db = d["t_dbg"].bitcast(F32)
                for i, src in enumerate((dtv, lndt, dAf, cum, rvc, d2e)):
                    p.st(db[i * 64:(i + 1) * 64, 0:W], src[0:64, :])
                p.barrier()

            for ch in range(NCH):
                cc0 = ch * 128
                gc0 = ch * 128 + c0
                zsel = 0
                if ch > 0 or tt > 0:
                    p.wait("tensor", ch_ac)
                    p.wait("tensor", ch_dv)
                    p.wait("vector", ch_ac)
                if zld_g.get(zsel) is not None:
                    p.wait("sync", zld_g[zsel])
                t_z = p.ld(zt[zsel][:], d["zs"][gc0:gc0 + 128, :])
                tks = []
                for (src, dst) in ((dAf, tmA), (lndt, tmL), (cum, tmE), (d2e, tmD)):
                    t_t = TRN(p, g.ps[7][:, 0:64], src[0:64, cc0:cc0 + 128], g.idn[0:64, 0:64])
                    p.wait("vector", t_t)
                    if dst is tmE:
                        p.wait("scalar", t_t)
                        t_c = ACT(p, dst[:], g.ps[7][:, 0:64], AF.Exp)
                        p.wait("vector", t_c)
                        p.wait("tensor", t_c)
                    else:
                        t_c = CP(p, dst[:], g.ps[7][:, 0:64])
                        p.wait("tensor", t_c)
                    tks.append(t_c)
                t_t = TRN(p, g.ps[7][0:1, 64:128], eend[0:64, ch:ch + 1],
                          g.idn[0:64, 0:64])
                p.wait("vector", t_t)
                t_er = CP(p, erow[:], g.ps[7][0:1, 64:128])
                p.wait("tensor", t_er)
                MM(p, g.ps[7][:, 128:160], mtop[:, :], erow[0:1, 0::2], True, False,
                   inc=False)
                t_hd = MM(p, g.ps[7][:, 128:160], mbot[:, :], erow[0:1, 1::2],
                          False, True)
                p.wait("vector", t_hd)
                t_hdc = CP(p, hdecp[:], g.ps[7][:, 128:160])
                p.wait("tensor", t_hdc)
                for par in range(2):
                    bk = g.ps[5 + par]
                    for q in range(8):
                        gi2 = 2 * q + par
                        pc = gi2 // 2
                        base = 64 * (gi2 % 2)
                        t_t = TRN(p, bk.bitcast(BF16)[:, q * 64:(q + 1) * 64],
                                  x_sb[base:base + 64, pc * W + cc0:pc * W + cc0 + 128],
                                  g.idnb[base:base + 64, base:base + 64])
                    p.wait("vector", t_t)
                    t_c = CP(p, x_tm[:].rearrange("p (gg n) -> p gg n", gg=16)[:, par::2, :],
                             bk.bitcast(BF16)[:, 0:512].rearrange("p (q n) -> p q n", q=8))
                    p.wait("tensor", t_c)
                for par in range(2):
                    bk = g.ps[5 + par]
                    for q in range(8):
                        gi2 = 2 * q + par
                        pc = gi2 // 2
                        base = 64 * (gi2 % 2)
                        t_t = TRN(p, bk.bitcast(BF16)[:, q * 64:(q + 1) * 64],
                                  b_sb[base:base + 64, pc * W + cc0:pc * W + cc0 + 128],
                                  g.idnb[base:base + 64, base:base + 64])
                    p.wait("vector", t_t)
                    t_c = CP(p, b_tm[:].rearrange("p (gg n) -> p gg n", gg=16)[:, par::2, :],
                             bk.bitcast(BF16)[:, 0:512].rearrange("p (q n) -> p q n", q=8))
                    p.wait("tensor", t_c)
                for par in range(2):
                    src_base = 64 * par
                    in_ap = b_sb[src_base:src_base + 64, :].rearrange(
                        "p (a w) -> p a w", a=8)[:, :, cc0:cc0 + 128]
                    out0 = bdup[0:64, :].rearrange("p (a j) -> p a j", a=16, j=128)[
                        :, par::2, :]
                    out1 = bdup[64:128, :].rearrange("p (a j) -> p a j", a=16, j=128)[
                        :, par::2, :]
                    CP(p, out0, in_ap)
                    t_bd = CP(p, out1, in_ap)
                p.wait("tensor", t_bd)

                prevg = {}
                for g4 in range(16):
                    if prevg:
                        p.wait("tensor", prevg["mx"])
                    t_diff = None
                    t_lda = p.dv(lambda e, o=lda4[:].rearrange("p (h i) -> p h i", h=4),
                                 a=g.ltri[:].unsqueeze(1).broadcast_to([128, 4, 128]),
                                 b=tmA[:, 4 * g4:4 * g4 + 4].unsqueeze(2).broadcast_to([128, 4, 128]):
                                 e.tensor_tensor(o, a, b, ALU.mult))
                    p.wait("vector", t_lda)
                    p.wait("tensor", t_lda)
                    for j in range(4):
                        MM(p, g.ps[0][:, j * 128:(j + 1) * 128], g.ustrictb[:],
                           lda4[:, j * 128:(j + 1) * 128], True, False, inc=False)
                        t_diff = MM(p, g.ps[0][:, j * 128:(j + 1) * 128], g.idnb[:],
                                    g.negbigb[:], False, True)
                    t_mx = None
                    p.wait("scalar", t_diff)
                    for j in range(4):
                        t_mx = ACT(p, mexp4[:, j * 128:(j + 1) * 128],
                                   g.ps[0][:, j * 128:(j + 1) * 128], AF.Exp,
                                   bias=tmL[:, 4 * g4 + j:4 * g4 + j + 1])
                    t_gt = {}
                    for par in range(2):
                        bk = g.ps[1 + par]
                        if prevg:
                            p.wait("tensor", prevg["gma" if par == 0 else "gmb"])
                        for l in range(2):
                            j = 2 * l + par
                            h = 4 * g4 + j
                            beta = 64 * (j % 2)
                            pc = h // 2
                            t_gt[par] = MM(p, bk[:, l * 128:(l + 1) * 128],
                                           bdup[beta:beta + 64, g4 * 128:(g4 + 1) * 128],
                                           c_sb[beta:beta + 64,
                                                pc * W + cc0:pc * W + cc0 + 128],
                                           True, True)
                    p.wait("vector", t_mx)
                    p.wait("vector", t_gt[0])
                    t_gma = TT(p, gma[:].rearrange("p (l i) -> p l i", l=2),
                               g.ps[1][:, 0:256].rearrange("p (l i) -> p l i", l=2),
                               mexp4[:].rearrange("p (h i) -> p h i", h=4)[:, 0::2, :],
                               ALU.mult)
                    p.wait("vector", t_gt[1])
                    t_gmb = TT(p, gmb[:].rearrange("p (l i) -> p l i", l=2),
                               g.ps[2][:, 0:256].rearrange("p (l i) -> p l i", l=2),
                               mexp4[:].rearrange("p (h i) -> p h i", h=4)[:, 1::2, :],
                               ALU.mult)
                    t_bd4 = p.dv(lambda e, o=bd4[:].rearrange("p (h n) -> p h n", h=4),
                                 a=b_tm[:, g4 * 64:(g4 + 1) * 64].unsqueeze(1).broadcast_to([128, 4, 64]),
                                 b=tmD[:, 4 * g4:4 * g4 + 4].unsqueeze(2).broadcast_to([128, 4, 64]):
                                 e.tensor_tensor(o, a, b, ALU.mult))
                    p.wait("vector", t_bd4)
                    p.wait("tensor", t_bd4)
                    p.wait("tensor", t_gma)
                    p.wait("tensor", t_gmb)
                    if prevg:
                        p.wait("tensor", prevg["t2"])
                    t_y1 = None
                    for j in range(4):
                        gm = (gma if j % 2 == 0 else gmb)
                        l = j // 2
                        t_y1 = MM(p, g.ps[3][:, j * 64:(j + 1) * 64],
                                  gm[:, l * 128:(l + 1) * 128],
                                  x_tm[:, g4 * 64:(g4 + 1) * 64], True, True)
                    if prevg:
                        p.wait("tensor", prevg["hu"])
                    t_s = None
                    for j in range(4):
                        beta = 64 * (j % 2)
                        l = j // 2
                        t_s = MM(p, g.ps[4][beta:beta + 64, l * 64:(l + 1) * 64],
                                 bd4[:, j * 64:(j + 1) * 64],
                                 x_tm[:, g4 * 64:(g4 + 1) * 64], True, True,
                                 tp=(0, beta))
                    t_hb = CP(p, hb16[:], Ht[:, g4 * 128:(g4 + 1) * 128])
                    p.wait("tensor", t_hb)
                    t_y2 = {}
                    for par in range(2):
                        bk = g.ps[5 + par]
                        if prevg:
                            p.wait("tensor", prevg["t1a" if par == 0 else "t1b"])
                        for l in range(2):
                            j = 2 * l + par
                            h = 4 * g4 + j
                            beta = 64 * (j % 2)
                            pc = h // 2
                            t_y2[par] = MM(p, bk[:, l * 64:(l + 1) * 64],
                                           c_sb[beta:beta + 64,
                                                pc * W + cc0:pc * W + cc0 + 128],
                                           hb16[beta:beta + 64, l * 64:(l + 1) * 64],
                                           True, True)
                    p.wait("vector", t_y1)
                    t_xd = TT(p, w1t[:].rearrange("p (h n) -> p h n", h=4),
                              x_tm[:, g4 * 64:(g4 + 1) * 64].unsqueeze(1).broadcast_to([128, 4, 64]),
                              dcol[:, 4 * g4:4 * g4 + 4].unsqueeze(2).broadcast_to([128, 4, 64]),
                              ALU.mult)
                    t_2 = TT(p, w2t[:], g.ps[3][:, 0:256], w1t[:], ALU.add)
                    p.wait("vector", t_y2[0])
                    t_1a = TT(p, w3a[:].rearrange("p (l n) -> p l n", l=2),
                              g.ps[5][:, 0:128].rearrange("p (l n) -> p l n", l=2),
                              tmE[:, 4 * g4:4 * g4 + 4].unsqueeze(2)
                              .broadcast_to([128, 4, 64])[:, 0::2, :], ALU.mult)
                    p.wait("vector", t_y2[1])
                    t_1b = TT(p, w3b[:].rearrange("p (l n) -> p l n", l=2),
                              g.ps[6][:, 0:128].rearrange("p (l n) -> p l n", l=2),
                              tmE[:, 4 * g4:4 * g4 + 4].unsqueeze(2)
                              .broadcast_to([128, 4, 64])[:, 1::2, :], ALU.mult)
                    TT(p, Yt[:, 4 * g4 * 64:(4 * g4 + 4) * 64]
                       .rearrange("p (h n) -> p h n", h=4)[:, 0::2, :],
                       w3a[:].rearrange("p (l n) -> p l n", l=2),
                       w2t[:].rearrange("p (h n) -> p h n", h=4)[:, 0::2, :], ALU.add)
                    t_yo = TT(p, Yt[:, 4 * g4 * 64:(4 * g4 + 4) * 64]
                              .rearrange("p (h n) -> p h n", h=4)[:, 1::2, :],
                              w3b[:].rearrange("p (l n) -> p l n", l=2),
                              w2t[:].rearrange("p (h n) -> p h n", h=4)[:, 1::2, :], ALU.add)
                    t_hm = TT(p, hmt[:].rearrange("p (m n) -> p m n", m=2),
                              Ht[:, g4 * 128:(g4 + 1) * 128].rearrange("p (m n) -> p m n", m=2),
                              hdecp[:, 2 * g4:2 * g4 + 2].unsqueeze(2)
                              .broadcast_to([128, 2, 64]),
                              ALU.mult)
                    p.wait("vector", t_s)
                    t_hu = TT(p, Ht[:, g4 * 128:(g4 + 1) * 128], hmt[:],
                              g.ps[4][:, 0:128], ALU.add)
                    prevg = {"mx": t_mx, "gma": t_gma, "gmb": t_gmb,
                             "t2": t_2, "hu": t_hu, "t1a": t_1a, "t1b": t_1b}
                    grp_dv = t_hu
                    grp_ac = t_mx
                    if DEBUG_TAP == "s4" and tt == 0 and ch == 0 and g4 == 0:
                        p.barrier()
                        db = d["t_dbg"]
                        p.st(db.bitcast(F32)[640:768, 0:512], lda4[:])
                        p.st(db[768:896, 0:512], mexp4[:])
                        p.st(db[896:1024, 0:256], gma[:])
                        p.st(db[896:1024, 256:512], gmb[:])
                        p.st(db[1024:1152, 0:1024], x_tm[:])
                        p.st(db[1152:1280, 0:1024], b_tm[:])
                        p.st(db[1280:1408, 0:2048], bdup[:])
                        p.st(db[1408:1536, 0:256], bd4[:])
                        p.barrier()
                if DEBUG_TAP == "s4" and tt == 0 and ch == 0:
                    p.barrier()
                    p.st(d["t_dbg"][1536:1664, 0:4096], Yt[:])
                    p.barrier()
                p.wait("vector", t_z)
                t_yz = TT(p, zt[zsel][:], zt[zsel][:], Yt[:], ALU.mult)
                zld_g[zsel] = t_yz
                p.wait("scalar", t_yz)
                t_sq4 = ACT(p, Yt[:], zt[zsel][:], AF.Square)
                p.wait("vector", t_sq4)
                t_ss = RED(p, ssq[:], Yt[:].rearrange("p (h n) -> p h n", h=64), ALU.add)
                p.wait("scalar", t_ss)
                ACT(p, rstd[:], ssq[:], AF.Ln, bias=1e-5, scale=1.0 / 64)
                t_rs = ACT(p, rstd[:], rstd[:], AF.Exp, scale=-0.5)
                p.wait("vector", t_rs)
                if DEBUG_TAP == "s4" and tt == 0 and ch == 0:
                    p.barrier()
                    p.st(d["t_dbg"].bitcast(F32)[1664:1792, 0:64], ssq[:])
                    p.st(d["t_dbg"].bitcast(F32)[1664:1792, 64:128], rstd[:])
                    p.barrier()
                t_nm = TT(p, Yt[:].rearrange("p (h n) -> p h n", h=64),
                          zt[zsel][:].rearrange("p (h n) -> p h n", h=64),
                          rstd[:].unsqueeze(2).broadcast_to([128, 64, 64]),
                          ALU.mult)
                zld_g[zsel] = t_nm
                p.wait("tensor", t_nm)
                for qb in range(8):
                    t_t = None
                    for qq in range(4):
                        ocn = qb * 4 + qq
                        t_t = TRN(p, g.ps[7].bitcast(BF16)[:, qq * 128:(qq + 1) * 128],
                                  Yt[:, ocn * 128:(ocn + 1) * 128], g.idnb[:])
                    p.wait("scalar", t_t)
                    t_ev = p.ac(lambda e, o=nrm[:].rearrange("p (c w) -> p c w", w=W)[
                        :, qb * 4:qb * 4 + 4, cc0:cc0 + 128],
                        i=g.ps[7].bitcast(BF16)[:, 0:512].rearrange(
                            "p (c j) -> p c j", c=4):
                        e.activation(o, i, AF.Copy))
                    p.wait("tensor", t_ev)
                grp_dv = t_nm
                grp_ac = t_ev
                ch_dv = t_nm
                ch_ac = t_ev
                if DEBUG_TAP == "s4" and tt == 0 and ch == 0:
                    p.barrier()
                    p.st(d["t_dbg"][384:512, 0:4096], Yt[:])
                    p.st(d["t_dbg"][512:640, 0:4096], zt[zsel][:])
                    p.barrier()
            p.barrier()

            dn = {}
            wo_g = None
            for ocn in range(NKC):
                bsel = ocn % 2
                if wo_g is not None:
                    p.wait("sync", wo_g)
                t_w = p.ld(wob[bsel][:].rearrange("p (c o) -> p c o", o=128),
                           d["wout_bf"][:, ocn * 128:(ocn + 1) * 128]
                           .rearrange("(c p) o -> p c o", p=128))
                bk = g.ps[ocn % 4]
                if ocn >= 4:
                    p.wait("tensor", dn[ocn - 4])
                p.wait("tensor", t_w)
                t_mm = mm_chain(p, bk[:, :],
                                [(wob[bsel][:, kc * 128:(kc + 1) * 128],
                                  nrm[:, kc * W:(kc + 1) * W]) for kc in range(32)])
                wo_g = t_mm
                p.wait("scalar", t_mm)
                t_e = ACT(p, delta[ocn], bk[:, :], AF.Copy)
                dn[ocn] = t_e
            p.barrier()

            st_tok = None
            for k in range(NKC):
                st_tok = ln_stats_chunk(p, g, delta[k], sqb, W, k == 0, k == NKC - 1,
                                        war=st_tok)
            bc = ln_rows_and_bcast(p, g, W, D, st_tok)
            for k in range(NKC):
                xr = xres[k % 2]
                t_x = p.ld(xr[:], d["x2"][k * 128:(k + 1) * 128, c0:c0 + W])
                p.wait("vector", t_x)
                t_n = ln_resid_chunk(p, g, xnew, delta[k], xr, scr, W, bc)
                p.wait("sync", t_n)
                t_st = p.st(d["x3"][k * 128:(k + 1) * 128, c0:c0 + W], xnew[:])
                p.wait("vector", t_st)
            p.barrier()



class LazyDram(dict):

    def __init__(self, nc, scratch_ext):
        super().__init__()
        self.nc = nc
        self.scratch_ext = scratch_ext

    def __missing__(self, name):
        if name == "x0":
            ap = self.nc.dram_tensor("x0", [D, T], F32, kind="ExternalInput").ap()
        elif name == "out":
            ap = self.nc.dram_tensor("out", [D, T], F32, kind="ExternalOutput").ap()
        elif name == "t_dbg":
            ap = self.nc.dram_tensor("t_dbg", [D, T], BF16, kind="ExternalOutput").ap()
        elif name in WSPECS:
            shape, dt = WSPECS[name]
            ap = self.nc.dram_tensor(name, shape, dt, kind="ExternalInput").ap()
        else:
            shape, dt = SCRATCH[name]
            kind = {"in": "ExternalInput", "out": "ExternalOutput"}.get(
                self.scratch_ext.get(name), "Internal")
            ap = self.nc.dram_tensor(name, shape, dt, kind=kind).ap()
        self[name] = ap
        return ap


def build_nc(stages, scratch_ext):
    nc = bass.Bass("TRN2", target_bir_lowering=False)
    d = LazyDram(nc, scratch_ext)
    with ExitStack() as es:
        p = Prog(nc, es)
        g = _setup_globals(p, d, es)
        for s in stages:
            s(p, g)
            p.barrier()
        p.check_deadlock()
        p.finalize()
    return nc


def _declared_inputs(nc):
    names = []
    for alloc in nc.m.functions[0].allocations:
        if isinstance(alloc, mybir.MemoryLocationSet) and alloc.kind == "ExternalInput":
            names.append(alloc.memorylocations[0].name)
    return names


def run_stages(stages, scratch_ext, per_core_maps):
    nc = build_nc(stages, scratch_ext)
    need = set(_declared_inputs(nc))
    in_maps = [{k: v for k, v in m.items() if k in need} for m in per_core_maps]
    res = run_bass_kernel_spmd(nc, in_maps, core_ids=list(range(len(in_maps))))
    return res.results


FUSED = True


def kernel(**inputs):
    w = _prep_weights(inputs)
    x = inputs["L0_attn_x"]
    ncores = 8
    per_core = []
    for b in range(ncores):
        m = dict(w)
        m["x0"] = np.ascontiguousarray(x[b].T.astype(np.float32))
        per_core.append(m)
    if FUSED:
        stages = [emit_s1, emit_s2, emit_s3, emit_s4, emit_s5a, emit_s5b]
        results = run_stages(stages, {}, per_core)
        return np.stack([r["out"].T for r in results], axis=0)
    launches = [
        ([emit_s1], {"x1": "out"}),
        ([emit_s2], {"x1": "in", "x2": "out"}),
        ([emit_s3], {"x2": "in", "zs": "out", "xbc": "out", "dtr": "out"}),
        ([emit_s4], {"x2": "in", "zs": "in", "xbc": "in", "dtr": "in",
                     "x3": "out"}),
        ([emit_s5a], {"x3": "in", "hxp": "out", "wfm": "out"}),
        ([emit_s5b], {"x3": "in", "hxp": "in", "wfm": "in"}),
    ]
    import gc
    drop_after = {1: ["x1"], 3: ["zs", "xbc", "dtr", "x2"], 5: ["x3", "hxp", "wfm"]}
    for li, (stages, ext) in enumerate(launches):
        results = run_stages(stages, ext, per_core)
        for b in range(ncores):
            for name, kind in ext.items():
                if kind == "out":
                    per_core[b][name] = results[b][name]
        for name in drop_after.get(li, []):
            for b in range(ncores):
                per_core[b].pop(name, None)
        if li < 5:
            results = None
        gc.collect()
    return np.stack([r["out"].T for r in results], axis=0)
